# version 09
# speedup vs baseline: 1.0789x; 1.0008x over previous
import contextlib
import math
import numpy as np
import ml_dtypes
import concourse.bass as bass
import concourse.mybir as mybir
from concourse.bass_utils import run_bass_kernel_spmd

F32 = mybir.dt.float32
BF16 = mybir.dt.bfloat16
I32 = mybir.dt.int32
AF = mybir.ActivationFunctionType
ALU = mybir.AluOpType
AX = mybir.AxisListType.X

C_VALS = [1e-5, 1e-6, 1.0, -1.0, 0.5]


def _host_consts(T):
    c = {}
    c["c_identf"] = np.eye(128, dtype=np.float32)
    c["c_identb"] = np.eye(128, dtype=np.float32).astype(ml_dtypes.bfloat16)
    hd = 64
    inv = 1.0 / (10000.0 ** (np.arange(0, hd, 2, dtype=np.float64) / hd))
    ang = np.arange(T, dtype=np.float64)[None, :] * inv[:, None]
    cos = np.cos(ang); sin = np.sin(ang)
    cos128 = np.tile(cos, (4, 1)); sin128 = np.tile(sin, (4, 1))
    c["c_ropeCq"] = (cos128 * 0.125).astype(ml_dtypes.bfloat16)
    c["c_ropeSq"] = (sin128 * 0.125).astype(ml_dtypes.bfloat16)
    c["c_ropeCk"] = cos128.astype(ml_dtypes.bfloat16)
    c["c_ropeSk"] = sin128.astype(ml_dtypes.bfloat16)
    P = np.zeros((128, 128), dtype=np.float32)
    for m in range(128):
        if m % 64 < 32:
            P[m + 32, m] = -1.0
        else:
            P[m - 32, m] = 1.0
    c["c_rotP"] = P.astype(ml_dtypes.bfloat16)
    bm = np.zeros((6, 128, 512), dtype=np.float32)
    for r in range(6):
        dlt = (r - 2) * 128
        kl = np.arange(128)[:, None]
        ql = np.arange(512)[None, :]
        d = dlt + kl - ql
        bm[r] = ((d <= 0) & (d >= -255)) / 512.0
    c["c_band"] = bm.astype(ml_dtypes.bfloat16)
    s = np.arange(128)[:, None]; t = np.arange(128)[None, :]
    c["c_tri"] = (s <= t).astype(np.float32).astype(ml_dtypes.bfloat16)
    c["c_h0"] = np.concatenate([np.ones(64), np.zeros(64)]).astype(np.float32)[None, :]
    c["c_h1"] = np.concatenate([np.zeros(64), np.ones(64)]).astype(np.float32)[None, :]
    cz = np.zeros((4, 128, 512), dtype=np.float32)
    for j in range(4):
        kl = np.arange(128)[:, None]
        ql = np.arange(512)[None, :]
        cz[j] = (128 * j + kl <= ql)
    c["c_caus4"] = cz.astype(ml_dtypes.bfloat16)
    c["c_scal"] = np.tile(np.asarray(C_VALS, np.float32)[None, :], (128, 1))
    return c


def _const_shapes(T):
    return {"c_identf": ((128, 128), F32), "c_identb": ((128, 128), BF16),
            "c_ropeCq": ((128, T), BF16), "c_ropeSq": ((128, T), BF16),
            "c_ropeCk": ((128, T), BF16), "c_ropeSk": ((128, T), BF16),
            "c_rotP": ((128, 128), BF16), "c_band": ((6, 128, 512), BF16),
            "c_tri": ((128, 128), BF16), "c_scal": ((128, len(C_VALS)), F32),
            "c_caus4": ((4, 128, 512), BF16),
            "c_h0": ((1, 128), F32), "c_h1": ((1, 128), F32)}



SEMS = (["PE", "DVE", "ACT", "POOL", "SP"]
        + ["g%d" % i for i in range(12)]
        + ["h%d" % i for i in range(8)]
        + ["a%d" % i for i in range(2)]
        + ["t%d" % i for i in range(4)] + ["gC"])


class Ctx:
    def __init__(self, nc, T):
        self.nc = nc
        self.T = T
        self.sem = {}
        self.c = {}
        self.ap = {}
        self.stack = None
        self.sched = {}
        self.recording = False

    def mksem(self, name):
        h = self.stack.enter_context(self.nc.semaphore(name))
        self.sem[name] = h
        self.c[name] = 0

    def inc(self, ins, name, n=1):
        ins.then_inc(self.sem[name], n)
        self.c[name] += n
        return self.c[name]

    def w(self, eng, name, val=None):
        eng.wait_ge(self.sem[name], self.c[name] if val is None else val)

    def fence(self, eng, name):
        eng.wait_ge(self.sem[name], self.c[name])

    def constf(self, v):
        return self.cscal[:, C_VALS.index(v):C_VALS.index(v) + 1]

    def snap(self, *names):
        if not names:
            names = SEMS
        return {k: self.c[k] for k in names}


def barrier(ctx, tag, skip=()):
    nc = ctx.nc
    snap = ctx.snap()
    for s in skip:
        snap.pop(s, None)
    with nc.Block("bar_" + tag) as block:
        for engname in ["gpsimd", "tensor", "vector", "scalar", "sync"]:
            def body(eng, snap=snap):
                for s, v in snap.items():
                    if v:
                        eng.wait_ge(ctx.sem[s], v)
            getattr(block, engname)(body)





class _MockIns:
    def then_inc(self, sem, n=1):
        return self


class _MockEng:
    def __getattr__(self, name):
        if name in ("wait_ge",):
            return lambda *a, **k: None
        return lambda *a, **k: _MockIns()


def run_two_pass(ctx, scripts):
    snap = dict(ctx.c)
    ctx.recording = True
    for kind, fn in scripts:
        fn(_MockEng())
    final = dict(ctx.c)
    ctx.c = dict(snap)
    ctx.recording = False
    with ctx.nc.Block() as block:
        for kind, fn in scripts:
            getattr(block, kind)(fn)
    assert ctx.c == final, "two-pass emission diverged"
    ctx.recording = False


def _mark(ctx, key, sem):
    if ctx.recording:
        ctx.sched[key] = (sem, ctx.c[sem])
    return None


def _need(ctx, eng, key, delta=0):
    if ctx.recording:
        return
    sem, val = ctx.sched[key]
    if val + delta > 0:
        eng.wait_ge(ctx.sem[sem], val + delta)


Ctx.mark = _mark
Ctx.need = _need



def _stream_transpose(ctx, src_is_tokens, tag=""):
    nc, T, ap = ctx.nc, ctx.T, ctx.ap
    ntt = T // 128
    c0 = ctx.snap("PE", "DVE", "h0", "h1", "g2", "g4")
    with (
        nc.sbuf_tensor("tr_in0" + tag, [128, 1024], F32) as tin0,
        nc.sbuf_tensor("tr_in1" + tag, [128, 1024], F32) as tin1,
        nc.sbuf_tensor("tr_out0" + tag, [128, 1024], F32) as tout0,
        nc.sbuf_tensor("tr_out1" + tag, [128, 1024], F32) as tout1,
        nc.psum_tensor("tr_ps0" + tag, [128, 512], F32) as ps0,
        nc.psum_tensor("tr_ps1" + tag, [128, 512], F32) as ps1,
        nc.Block() as block,
    ):
        tins, touts, pss, dsem = [tin0, tin1], [tout0, tout1], [ps0, ps1], ["h0", "h1"]

        @block.sync
        def _(s):
            for tt in range(ntt):
                if tt >= 2:
                    s.wait_ge(ctx.sem["PE"], c0["PE"] + 8 * (tt - 1))
                if src_is_tokens:
                    i = ap["x_shard"][tt * 128:(tt + 1) * 128, :]
                    o = tins[tt % 2][:]
                else:
                    i = ap["X"][:, :, tt * 128:(tt + 1) * 128].rearrange("c p n -> p c n")
                    o = tins[tt % 2][:].rearrange("p (c n) -> p c n", n=128)
                ctx.inc(s.dma_start(o, i), dsem[tt % 2], 16)

        @block.tensor
        def _(t):
            for tt in range(ntt):
                t.wait_ge(ctx.sem[dsem[tt % 2]],
                          c0[dsem[tt % 2]] + 16 * (tt // 2 + 1))
                for dc in range(8):
                    idx = 8 * tt + dc
                    if idx >= 2:
                        t.wait_ge(ctx.sem["DVE"], c0["DVE"] + idx - 1)
                    ins = t.transpose(pss[idx % 2][:, 0:128],
                                      tins[tt % 2][:, dc * 128:(dc + 1) * 128],
                                      ctx.cident_f)
                    ctx.inc(ins, "PE", 1)

        ssem = ["g2", "g4"]

        @block.vector
        def _(v):
            for tt in range(ntt):
                if tt >= 2:
                    v.wait_ge(ctx.sem[ssem[tt % 2]],
                              c0[ssem[tt % 2]] + 16 * ((tt - 2) // 2 + 1))
                for dc in range(8):
                    idx = 8 * tt + dc
                    v.wait_ge(ctx.sem["PE"], c0["PE"] + idx + 1)
                    ins = v.tensor_copy(touts[tt % 2][:, dc * 128:(dc + 1) * 128],
                                        pss[idx % 2][:, 0:128])
                    ctx.inc(ins, "DVE", 1)

        @block.gpsimd
        def _(g):
            for tt in range(ntt):
                g.wait_ge(ctx.sem["DVE"], c0["DVE"] + 8 * (tt + 1))
                if src_is_tokens:
                    o = ap["X"][:, :, tt * 128:(tt + 1) * 128].rearrange("c p n -> p c n")
                    i = touts[tt % 2][:].rearrange("p (c n) -> p c n", n=128)
                else:
                    o = ap["out_shard"][tt * 128:(tt + 1) * 128, :]
                    i = touts[tt % 2][:]
                ctx.inc(g.dma_start(o, i), ssem[tt % 2], 16)
            g.wait_ge(ctx.sem["g2"], ctx.c["g2"])
            g.wait_ge(ctx.sem["g4"], ctx.c["g4"])



def p0_prep(ctx, prep_moe=True, prep_l1=True):
    nc, ap = ctx.nc, ctx.ap
    _stream_transpose(ctx, src_is_tokens=True, tag="i")
    barrier(ctx, "p0a")
    if not (prep_moe or prep_l1):
        return

    with nc.Block() as block:
        @block.gpsimd
        def _(g):
            if prep_moe:
                ctx.inc(g.dma_start(ap["s_wqc"], ap["L0_ffn_wq"]), "g5", 16)
                ctx.inc(g.dma_start(ap["s_woc"], ap["L0_ffn_wo"]), "g5", 16)
                ctx.inc(g.dma_start(ap["s_wkc"], ap["L0_ffn_wk"]), "g5", 16)
                ctx.inc(g.dma_start(ap["s_wvc"], ap["L0_ffn_wv"]), "g5", 16)
            if prep_l1:
                for src, dst in (("L1_ffn_w_gate", "s_wg"), ("L1_ffn_w_up", "s_wu"),
                                 ("L1_ffn_w_down", "s_wd")):
                    ctx.inc(g.dma_start(ap[dst], ap[src]), "g3", 16)
                ctx.inc(g.dma_start(ap["s_ip"], ap["L1_attn_in_proj_w"]), "g3", 16)
                ctx.inc(g.dma_start(ap["s_op"], ap["L1_attn_out_proj_w"]), "g3", 16)
            if prep_moe and getattr(ctx, "cast_moe_at", None) is None:
                for e in range(8):
                    for src, dst in (("L0_ffn_w_gate", "s_g"), ("L0_ffn_w_up", "s_u"),
                                     ("L0_ffn_w_down", "s_d")):
                        ctx.inc(g.dma_start(ap[dst][e], ap[src][e]), "gC", 16)
    if prep_moe:
        ctx.sched[("p0", "attnw")] = ("g5", ctx.c["g5"])
    if prep_moe and getattr(ctx, "cast_moe_at", None) is None:
        ctx.sched[("p0", "moe")] = ("gC", ctx.c["gC"])
    if not prep_l1:
        barrier(ctx, "p0c", skip=("gC",))
        return

    g3need = ctx.c["g3"]
    NS, LAG = 4, 3
    jobs = []
    for src, dst, rows, cols in [
            ("s_wg", "s_wgT", 3072, 1024), ("s_wu", "s_wuT", 3072, 1024),
            ("s_wd", "s_wdT", 1024, 3072),
            ("s_ip", "s_ipT", 4384, 1024), ("s_op", "s_opT", 1024, 2048)]:
        for col0 in range(0, cols, 128):
            jobs.append((src, dst, rows, col0))
    with (
        nc.sbuf_tensor("p0_tr", [128, NS * 4384], BF16) as ttr,
        nc.Block() as block,
    ):
        @block.sync
        def _(s):
            s.wait_ge(ctx.sem["g3"], g3need)
            for i in range(len(jobs) + LAG):
                if i < len(jobs):
                    src, dst, rows, col0 = jobs[i]
                    sl = i % NS
                    ts = "t%d" % sl
                    if i >= NS:
                        ctx.w(s, ts)
                    ctx.inc(s.dma_start(ttr[:, sl * 4384:sl * 4384 + rows],
                                        ap[src][:, col0:col0 + 128],
                                        transpose=True), ts, 16)
                j = i - LAG
                if j >= 0:
                    src, dst, rows, col0 = jobs[j]
                    sl = j % NS
                    ts = "t%d" % sl
                    ctx.w(s, ts)
                    ctx.inc(s.dma_start(ap[dst][col0:col0 + 128, :],
                                        ttr[:, sl * 4384:sl * 4384 + rows]),
                            ts, 16)
    barrier(ctx, "p0c", skip=("gC",))



def p_l1ffn(ctx, tag):
    nc, T, ap = ctx.nc, ctx.T, ctx.ap
    ntc = T // 512
    NHC = 24
    s127 = 1.0 / 127.0
    P_DVE, P_ACT, P_PE = 60, 33, 594
    c0 = ctx.snap()

    def base(e, tc):
        return c0[e] + tc * {"DVE": P_DVE, "ACT": P_ACT, "PE": P_PE}[e]

    with contextlib.ExitStack() as st:
        def sb(name, shape, dt):
            return st.enter_context(nc.sbuf_tensor(tag + name, shape, dt))
        def pt(name):
            return st.enter_context(nc.psum_tensor(tag + name, [128, 512], F32))
        Xt = sb("_Xt", [128, 8 * 512], F32)
        xbf = sb("_xbf", [128, 8 * 512], BF16)
        xsq = sb("_xsq", [128, 8 * 512], BF16)
        gsb = sb("_g", [128, 8 * 512], BF16)
        act = sb("_act", [128, NHC * 512], BF16)
        wgs = [sb("_wg%d" % i, [128, 8 * 128], BF16) for i in range(4)]
        wus = [sb("_wu%d" % i, [128, 8 * 128], BF16) for i in range(4)]
        wd0 = sb("_wd0", [128, 24 * 128], BF16); wd1 = sb("_wd1", [128, 24 * 128], BF16)
        sil0 = sb("_sil0", [128, 512], BF16); sil1 = sb("_sil1", [128, 512], BF16)
        tmp0 = sb("_tmp0", [128, 512], F32)
        tmp1 = sb("_tmp1", [128, 512], F32)
        rows = sb("_rows", [1, 5 * 512], F32)
        pS1 = pt("_pS1"); pS2 = pt("_pS2"); pB1 = pt("_pB1"); pB2 = pt("_pB2")
        pG = pt("_pG"); pU = pt("_pU"); pD0 = pt("_pD0"); pD1 = pt("_pD1")
        block = st.enter_context(nc.Block())
        wds, sils = [wd0, wd1], [sil0, sil1]
        pGs, pUs, pDs = [pG, pS1, pB1], [pU, pS2, pB2], [pD0, pD1]
        GSEM = ["g5", "g6", "g2", "g8"]
        USEM = ["g0", "g1", "g4", "g10"]
        r_mu = rows[:, 0:512]
        r_m2 = rows[:, 512:1024]
        r_var = rows[:, 1024:1536]
        r_r = rows[:, 1536:2048]
        r_sq = rows[:, 2048:2560]

        @block.sync
        def _(s):
            for tc in range(ntc):
                if tc >= 1:
                    s.wait_ge(ctx.sem["g7"], c0["g7"] + 16 * tc)
                i = ap["X"][:, :, tc * 512:(tc + 1) * 512].rearrange("c p n -> p c n")
                ctx.inc(s.dma_start(Xt[:].rearrange("p (c n) -> p c n", n=512), i),
                        ["h0", "h1"][tc % 2], 16)
                for dc in range(8):
                    if tc > 0 or dc >= 2:
                        pe_need = base("PE", tc) + 402 + 24 * (dc - 1) if dc >= 2 \
                            else base("PE", tc - 1) + 402 + 24 * (dc + 7)
                        s.wait_ge(ctx.sem["PE"], pe_need)
                    w3 = ap["s_wdT"][:, dc * 128:(dc + 1) * 128] \
                        .rearrange("(c p) d -> p c d", p=128)
                    ctx.inc(s.dma_start(
                        wds[dc % 2][:].rearrange("p (c d) -> p c d", d=128), w3),
                        ["h4", "h5"][dc % 2], 16)

        @block.gpsimd
        def _(g):
            for tc in range(ntc):
                for hc in range(NHC):
                    if tc > 0 or hc >= 4:
                        prev = hc - 4 if hc >= 4 else hc + NHC - 4
                        ptc = tc if hc >= 4 else tc - 1
                        g.wait_ge(ctx.sem["PE"],
                                  base("PE", ptc) + 18 + 16 * (prev + 1))
                    wg3 = ap["s_wgT"][:, hc * 128:(hc + 1) * 128] \
                        .rearrange("(c p) m -> p c m", p=128)
                    ctx.inc(g.dma_start(
                        wgs[hc % 4][:].rearrange("p (c m) -> p c m", m=128), wg3),
                        GSEM[hc % 4], 16)
                    wu3 = ap["s_wuT"][:, hc * 128:(hc + 1) * 128] \
                        .rearrange("(c p) m -> p c m", p=128)
                    ctx.inc(g.dma_start(
                        wus[hc % 4][:].rearrange("p (c m) -> p c m", m=128), wu3),
                        USEM[hc % 4], 16)
                g.wait_ge(ctx.sem["DVE"], base("DVE", tc) + P_DVE)
                o = ap["X"][:, :, tc * 512:(tc + 1) * 512].rearrange("c p n -> p c n")
                ctx.inc(g.dma_start(o, Xt[:].rearrange("p (c n) -> p c n", n=512)),
                        "g7", 16)
            g.wait_ge(ctx.sem["g7"], ctx.c["g7"])

        @block.vector
        def _(v):
            for tc in range(ntc):
                dv = base("DVE", tc)
                ac = base("ACT", tc)
                pe = base("PE", tc)
                v.wait_ge(ctx.sem[["h0", "h1"][tc % 2]],
                          c0[["h0", "h1"][tc % 2]] + 16 * (tc // 2 + 1))
                for dc in range(8):
                    ctx.inc(v.tensor_copy(xbf[:, dc * 512:(dc + 1) * 512],
                                          Xt[:, dc * 512:(dc + 1) * 512]), "DVE")
                v.wait_ge(ctx.sem["PE"], pe + 16)
                ctx.inc(v.tensor_scalar_mul(r_mu, pS1[0:1, :], 1.0 / 1024), "DVE")
                ctx.fence(v, "DVE")
                ctx.inc(v.tensor_tensor(r_m2, r_mu, r_mu, ALU.mult), "DVE")
                ctx.fence(v, "DVE")
                ctx.inc(v.scalar_tensor_tensor(r_var, pS2[0:1, :], 1.0 / 1024,
                                               r_m2, ALU.mult, ALU.subtract), "DVE")
                v.wait_ge(ctx.sem["ACT"], ac + 9)
                ctx.inc(v.reciprocal(r_r, r_sq), "DVE")
                v.wait_ge(ctx.sem["PE"], pe + 18)
                for dc in range(8):
                    tmp = [tmp0, tmp1][dc % 2]
                    if dc >= 2:
                        ctx.fence(v, "DVE")
                    ctx.inc(v.scalar_tensor_tensor(
                        tmp[:], Xt[:, dc * 512:(dc + 1) * 512], 1.0, pB1[:],
                        ALU.mult, ALU.subtract), "DVE")
                    ctx.fence(v, "DVE")
                    ctx.inc(v.scalar_tensor_tensor(
                        gsb[:, dc * 512:(dc + 1) * 512], tmp[:], 1.0, pB2[:],
                        ALU.mult, ALU.mult), "DVE")
                for hc in range(NHC):
                    v.wait_ge(ctx.sem["ACT"], ac + 9 + (hc + 1))
                    v.wait_ge(ctx.sem["PE"], pe + 18 + 16 * (hc + 1))
                    ctx.inc(v.tensor_tensor(act[:, hc * 512:(hc + 1) * 512],
                                            sils[hc % 2][:], pUs[hc % 3][:],
                                            ALU.mult), "DVE")
                for dc in range(8):
                    v.wait_ge(ctx.sem["PE"], pe + 402 + 24 * (dc + 1))
                    ctx.inc(v.scalar_tensor_tensor(
                        Xt[:, dc * 512:(dc + 1) * 512], pDs[dc % 2][:],
                        0.5 * s127 * s127, Xt[:, dc * 512:(dc + 1) * 512],
                        ALU.mult, ALU.add), "DVE")

        @block.scalar
        def _(sc):
            for tc in range(ntc):
                dv = base("DVE", tc)
                pe = base("PE", tc)
                sc.wait_ge(ctx.sem[["h0", "h1"][tc % 2]],
                           c0[["h0", "h1"][tc % 2]] + 16 * (tc // 2 + 1))
                for dc in range(8):
                    ctx.inc(sc.activation(xsq[:, dc * 512:(dc + 1) * 512],
                                          Xt[:, dc * 512:(dc + 1) * 512],
                                          AF.Square), "ACT")
                sc.wait_ge(ctx.sem["DVE"], dv + 11)
                ctx.inc(sc.activation(r_sq, r_var, AF.Sqrt, bias=ctx.constf(1e-5)[0:1, :]), "ACT")
                for hc in range(NHC):
                    sc.wait_ge(ctx.sem["PE"], pe + 18 + 16 * hc + 8)
                    if hc >= 2:
                        sc.wait_ge(ctx.sem["DVE"], dv + 28 + (hc - 1))
                    ctx.inc(sc.activation(sils[hc % 2][:], pGs[hc % 3][:],
                                          AF.Silu, scale=s127), "ACT")

        @block.tensor
        def _(t):
            for tc in range(ntc):
                dv = base("DVE", tc)
                ac = base("ACT", tc)
                t.wait_ge(ctx.sem["DVE"], dv + 8)
                t.wait_ge(ctx.sem["ACT"], ac + 8)
                for dc in range(8):
                    ctx.inc(t.matmul(pS1[0:1, :], ctx.ones_b,
                                     xbf[:, dc * 512:(dc + 1) * 512],
                                     start=(dc == 0), stop=(dc == 7)), "PE")
                    ctx.inc(t.matmul(pS2[0:1, :], ctx.ones_b,
                                     xsq[:, dc * 512:(dc + 1) * 512],
                                     start=(dc == 0), stop=(dc == 7)), "PE")
                t.wait_ge(ctx.sem["DVE"], dv + 9)
                ctx.inc(t.matmul(pB1[:], ctx.ones_f, r_mu, start=True, stop=True),
                        "PE")
                t.wait_ge(ctx.sem["DVE"], dv + 12)
                ctx.inc(t.matmul(pB2[:], ctx.ones_f, r_r, start=True, stop=True),
                        "PE")
                t.wait_ge(ctx.sem["DVE"], dv + 28)
                for hc in range(NHC):
                    t.wait_ge(ctx.sem[GSEM[hc % 4]],
                              c0[GSEM[hc % 4]] + 16 * (tc * 6 + hc // 4 + 1))
                    t.wait_ge(ctx.sem[USEM[hc % 4]],
                              c0[USEM[hc % 4]] + 16 * (tc * 6 + hc // 4 + 1))
                    if tc > 0 or hc >= 3:
                        prev = hc - 3 if hc >= 3 else hc + NHC - 3
                        ptc = tc if hc >= 3 else tc - 1
                        t.wait_ge(ctx.sem["ACT"],
                                  base("ACT", ptc) + 9 + (prev + 1))
                        t.wait_ge(ctx.sem["DVE"],
                                  base("DVE", ptc) + 28 + (prev + 1))
                    for kk in range(8):
                        ctx.inc(t.matmul(pGs[hc % 3][:],
                                         wgs[hc % 4][:, kk * 128:(kk + 1) * 128],
                                         gsb[:, kk * 512:(kk + 1) * 512],
                                         start=(kk == 0), stop=(kk == 7)), "PE")
                    for kk in range(8):
                        ctx.inc(t.matmul(pUs[hc % 3][:],
                                         wus[hc % 4][:, kk * 128:(kk + 1) * 128],
                                         gsb[:, kk * 512:(kk + 1) * 512],
                                         start=(kk == 0), stop=(kk == 7)), "PE")
                for dc in range(8):
                    t.wait_ge(ctx.sem[["h4", "h5"][dc % 2]],
                              c0[["h4", "h5"][dc % 2]] + 16 * (tc * 4 + dc // 2 + 1))
                    if dc == 0:
                        t.wait_ge(ctx.sem["DVE"], dv + 52)
                    if tc > 0 or dc >= 2:
                        dv_need = dv + 52 + (dc - 1) if dc >= 2 \
                            else base("DVE", tc - 1) + 52 + (dc + 7)
                        t.wait_ge(ctx.sem["DVE"], dv_need)
                    for hc in range(NHC):
                        ctx.inc(t.matmul(pDs[dc % 2][:],
                                         wds[dc % 2][:, hc * 128:(hc + 1) * 128],
                                         act[:, hc * 512:(hc + 1) * 512],
                                         start=(hc == 0), stop=(hc == NHC - 1)),
                                "PE")



def p_l0attn(ctx):
    nc, T, ap = ctx.nc, ctx.T, ctx.ap
    nqc = T // 512

    def NK(qc):
        return 4 if qc == 0 else 6

    def kts(qc):
        return [(r, 4 * qc + (r - 2)) for r in range(6) if 4 * qc + (r - 2) >= 0]

    def pe_n(qc):
        return 229 + 17 * NK(qc)

    def dve_n2(qc):
        return 73 + NK(qc)

    def act_n(qc):
        return 82

    c0 = ctx.snap()
    peb = [c0["PE"]]
    dvb = [c0["DVE"]]
    acb = [c0["ACT"]]
    for qc in range(nqc):
        peb.append(peb[-1] + pe_n(qc))
        dvb.append(dvb[-1] + dve_n2(qc))
        acb.append(acb[-1] + act_n(qc))

    def pe_off(qc):
        nk = NK(qc)
        return {"stats": 0, "bcast": 16, "qproj": 18, "kproj": 82, "vproj": 146,
                "S": 210, "rowsum": 210 + 8 * nk, "brs": 210 + 9 * nk,
                "out": 211 + 9 * nk, "stats2": 211 + 17 * nk,
                "bcast2": 227 + 17 * nk, "end": 229 + 17 * nk}

    def dv_off(qc):
        nk = NK(qc)
        return {"xbf": 0, "rows": 8, "gapply": 12, "smask": 28, "rsrow": 28 + nk,
                "attn": 29 + nk, "xbf2": 37 + nk, "rows2": 45 + nk,
                "apply2": 49 + nk, "resid": 65 + nk, "end": 73 + nk}

    def ac_off(qc):
        return {"sq": 0, "sqrt": 8, "qevac": 9, "kevac": 17, "vevac": 49,
                "oevac": 57, "sq2": 65, "sqrt2": 73, "lnb": 74, "end": 82}

    with contextlib.ExitStack() as st:
        def sb(name, shape, dt):
            return st.enter_context(nc.sbuf_tensor("a2" + name, shape, dt))

        def pt(name):
            return st.enter_context(nc.psum_tensor("a2" + name, [128, 512], F32))

        Xt = sb("_Xt", [128, 8 * 512], F32)
        xbf = sb("_xbf", [128, 8 * 512], BF16)
        xsq = sb("_xsq", [128, 8 * 512], BF16)
        gsb = sb("_g", [128, 8 * 512], BF16)
        qsb = sb("_q", [128, 8 * 512], BF16)
        kring = sb("_kr", [128, 8 * 6 * 128], BF16)
        vring = sb("_vr", [128, 6 * 1024], BF16)
        ssb = sb("_s", [128, 6 * 512], BF16)
        osb = sb("_o", [128, 512], F32)
        osb2 = sb("_o2", [128, 512], F32)
        attn = sb("_at", [128, 8 * 512], F32)
        wqkv = sb("_w", [128, 8 * 3072], BF16)
        bq = sb("_bq", [128, 8], F32)
        bk = sb("_bk", [128, 8], F32)
        bv = sb("_bv", [128, 8], F32)
        lnw = sb("_lnw", [128, 8], F32)
        lnb = sb("_lnb", [128, 8], F32)
        band = sb("_band", [128, 6 * 512], BF16)
        rows = sb("_rows", [1, 6 * 512], F32)
        rs_row = sb("_rs", [1, 512], F32)[:]
        tmp0 = sb("_tmp0", [128, 512], F32)
        tmp1 = sb("_tmp1", [128, 512], F32)
        y2 = sb("_y2", [128, 8 * 512], F32)
        pS1 = pt("_pS1"); pS2 = pt("_pS2"); pB1 = pt("_pB1"); pB2 = pt("_pB2")
        pA = pt("_pA"); pB_ = pt("_pB"); pO0 = pt("_pO0"); pO1 = pt("_pO1")
        block = st.enter_context(nc.Block())
        pps = [pA, pB_]
        pos = [pO0, pO1]
        r_mu = rows[:, 0:512]; r_m2 = rows[:, 512:1024]; r_var = rows[:, 1024:1536]
        r_r = rows[:, 1536:2048]; r_sq = rows[:, 2048:2560]

        @block.gpsimd
        def _(g):
            ctx.inc(g.dma_start(wqkv[:].rearrange("p (c n) -> p c n", n=3072),
                                ap["L0_attn_W_qkv"].rearrange("(c p) n -> p c n", p=128)),
                    "g3", 16)
            for vap, dst in ((ap["L0_attn_b_qkv"][0:1024], bq),
                             (ap["L0_attn_b_qkv"][1024:2048], bk),
                             (ap["L0_attn_b_qkv"][2048:3072], bv),
                             (ap["L0_attn_ln_w"], lnw), (ap["L0_attn_ln_b"], lnb)):
                ctx.inc(g.dma_start(dst[:], vap.rearrange("(c p) -> p c", p=128),
                                    allow_slow_non_contiguous=True), "g3", 16)
            ctx.inc(g.dma_start(band[:].rearrange("p (r n) -> p r n", n=512),
                                ap["c_band"].rearrange("r p n -> p r n")), "g3", 16)
            for qc in range(nqc):
                g.wait_ge(ctx.sem["DVE"], dvb[qc] + dv_off(qc)["resid"] + 8)
                o = ap["X"][:, :, qc * 512:(qc + 1) * 512].rearrange("c p n -> p c n")
                ctx.inc(g.dma_start(o, Xt[:].rearrange("p (c n) -> p c n", n=512)),
                        "g7", 16)
            g.wait_ge(ctx.sem["g7"], ctx.c["g7"])

        @block.sync
        def _(s):
            for qc in range(nqc):
                if qc >= 1:
                    s.wait_ge(ctx.sem["g7"], c0["g7"] + 16 * qc)
                i = ap["X"][:, :, qc * 512:(qc + 1) * 512].rearrange("c p n -> p c n")
                ctx.inc(s.dma_start(Xt[:].rearrange("p (c n) -> p c n", n=512), i),
                        ["h0", "h1"][qc % 2], 16)

        @block.vector
        def _(v):
            for qc in range(nqc):
                dvo = dv_off(qc)
                peo = pe_off(qc)
                v.wait_ge(ctx.sem[["h0", "h1"][qc % 2]],
                          c0[["h0", "h1"][qc % 2]] + 16 * (qc // 2 + 1))
                if qc >= 1:
                    v.wait_ge(ctx.sem["PE"], peb[qc])
                for dc in range(8):
                    ctx.inc(v.tensor_copy(xbf[:, dc * 512:(dc + 1) * 512],
                                          Xt[:, dc * 512:(dc + 1) * 512]), "DVE")
                v.wait_ge(ctx.sem["PE"], peb[qc] + 16)
                ctx.inc(v.tensor_scalar_mul(r_mu, pS1[0:1, :], 1.0 / 1024), "DVE")
                ctx.fence(v, "DVE")
                ctx.inc(v.tensor_tensor(r_m2, r_mu, r_mu, ALU.mult), "DVE")
                ctx.fence(v, "DVE")
                ctx.inc(v.scalar_tensor_tensor(r_var, pS2[0:1, :], 1.0 / 1024,
                                               r_m2, ALU.mult, ALU.subtract), "DVE")
                v.wait_ge(ctx.sem["ACT"], acb[qc] + 9)
                ctx.inc(v.reciprocal(r_r, r_sq), "DVE")
                v.wait_ge(ctx.sem["PE"], peb[qc] + 18)
                for dc in range(8):
                    tmp = [tmp0, tmp1][dc % 2]
                    if dc >= 2:
                        ctx.fence(v, "DVE")
                    ctx.inc(v.scalar_tensor_tensor(
                        tmp[:], Xt[:, dc * 512:(dc + 1) * 512], 1.0, pB1[:],
                        ALU.mult, ALU.subtract), "DVE")
                    ctx.fence(v, "DVE")
                    ctx.inc(v.scalar_tensor_tensor(
                        gsb[:, dc * 512:(dc + 1) * 512], tmp[:], 1.0, pB2[:],
                        ALU.mult, ALU.mult), "DVE")
                for j, (r, kt) in enumerate(kts(qc)):
                    v.wait_ge(ctx.sem["PE"], peb[qc] + peo["S"] + 8 * (j + 1))
                    ctx.inc(v.tensor_tensor(ssb[:, r * 512:(r + 1) * 512],
                                            pps[j % 2][:],
                                            band[:, r * 512:(r + 1) * 512],
                                            ALU.mult), "DVE")
                v.wait_ge(ctx.sem["PE"], peb[qc] + peo["brs"])
                ctx.inc(v.tensor_copy(rs_row, pB1[0:1, :]), "DVE")
                for dc in range(8):
                    v.wait_ge(ctx.sem["PE"], peb[qc] + peo["brs"] + 1)
                    v.wait_ge(ctx.sem["ACT"], acb[qc] + ac_off(qc)["oevac"] + dc + 1)
                    ctx.inc(v.scalar_tensor_tensor(
                        attn[:, dc * 512:(dc + 1) * 512], pB2[:], bv[:, dc:dc + 1],
                        [osb, osb2][dc % 2][:], ALU.mult, ALU.add), "DVE")
                for dc in range(8):
                    ctx.fence(v, "DVE")
                    ctx.inc(v.tensor_copy(xbf[:, dc * 512:(dc + 1) * 512],
                                          attn[:, dc * 512:(dc + 1) * 512]), "DVE")
                v.wait_ge(ctx.sem["PE"], peb[qc] + peo["stats2"] + 16)
                ctx.inc(v.tensor_scalar_mul(r_mu, pS1[0:1, :], 1.0 / 1024), "DVE")
                ctx.fence(v, "DVE")
                ctx.inc(v.tensor_tensor(r_m2, r_mu, r_mu, ALU.mult), "DVE")
                ctx.fence(v, "DVE")
                ctx.inc(v.scalar_tensor_tensor(r_var, pS2[0:1, :], 1.0 / 1024,
                                               r_m2, ALU.mult, ALU.subtract), "DVE")
                v.wait_ge(ctx.sem["ACT"], acb[qc] + ac_off(qc)["sqrt2"] + 1)
                ctx.inc(v.reciprocal(r_r, r_sq), "DVE")
                v.wait_ge(ctx.sem["PE"], peb[qc] + peo["bcast2"] + 2)
                for dc in range(8):
                    tmp = [tmp0, tmp1][dc % 2]
                    if dc >= 2:
                        ctx.fence(v, "DVE")
                    ctx.inc(v.scalar_tensor_tensor(
                        tmp[:], attn[:, dc * 512:(dc + 1) * 512], 1.0, pB1[:],
                        ALU.mult, ALU.subtract), "DVE")
                    ctx.fence(v, "DVE")
                    ctx.inc(v.scalar_tensor_tensor(
                        y2[:, dc * 512:(dc + 1) * 512], tmp[:], lnw[:, dc:dc + 1],
                        pB2[:], ALU.mult, ALU.mult), "DVE")
                for dc in range(8):
                    v.wait_ge(ctx.sem["ACT"], acb[qc] + ac_off(qc)["lnb"] + dc + 1)
                    ctx.inc(v.tensor_tensor(Xt[:, dc * 512:(dc + 1) * 512],
                                            Xt[:, dc * 512:(dc + 1) * 512],
                                            y2[:, dc * 512:(dc + 1) * 512],
                                            ALU.add), "DVE")

        @block.scalar
        def _(sc):
            for qc in range(nqc):
                dvo = dv_off(qc)
                peo = pe_off(qc)
                aco = ac_off(qc)
                sc.wait_ge(ctx.sem[["h0", "h1"][qc % 2]],
                           c0[["h0", "h1"][qc % 2]] + 16 * (qc // 2 + 1))
                if qc >= 1:
                    sc.wait_ge(ctx.sem["PE"], peb[qc])
                for dc in range(8):
                    ctx.inc(sc.activation(xsq[:, dc * 512:(dc + 1) * 512],
                                          Xt[:, dc * 512:(dc + 1) * 512],
                                          AF.Square), "ACT")
                sc.wait_ge(ctx.sem["DVE"], dvb[qc] + dvo["rows"] + 3)
                ctx.inc(sc.activation(r_sq, r_var, AF.Sqrt,
                                      bias=ctx.constf(1e-5)[0:1, :]), "ACT")
                for oc in range(8):
                    sc.wait_ge(ctx.sem["PE"], peb[qc] + peo["qproj"] + 8 * (oc + 1))
                    ctx.inc(sc.add(qsb[:, oc * 512:(oc + 1) * 512],
                                   pps[oc % 2][:], bq[:, oc:oc + 1]), "ACT")
                for oc in range(8):
                    sc.wait_ge(ctx.sem["PE"], peb[qc] + peo["kproj"] + 8 * (oc + 1))
                    for j in range(4):
                        slot = (4 * qc + j) % 6
                        ctx.inc(sc.add(
                            kring[:, (oc * 6 + slot) * 128:(oc * 6 + slot) * 128 + 128],
                            pps[oc % 2][:, j * 128:(j + 1) * 128],
                            bk[:, oc:oc + 1]), "ACT")
                for tt in range(4):
                    slot = (4 * qc + tt) % 6
                    for hf in range(2):
                        sc.wait_ge(ctx.sem["PE"],
                                   peb[qc] + peo["vproj"] + 8 * (2 * tt + hf + 1))
                        ctx.inc(sc.copy(
                            vring[:, slot * 1024 + hf * 512:slot * 1024 + hf * 512 + 512],
                            pps[(2 * tt + hf) % 2][:]), "ACT")
                for dc in range(8):
                    sc.wait_ge(ctx.sem["PE"],
                               peb[qc] + peo["out"] + NK(qc) * (dc + 1))
                    if dc >= 2:
                        sc.wait_ge(ctx.sem["DVE"], dvb[qc] + dvo["attn"] + dc - 1)
                    ctx.inc(sc.copy([osb, osb2][dc % 2][:], pos[dc % 2][:]), "ACT")
                for dc in range(8):
                    sc.wait_ge(ctx.sem["DVE"], dvb[qc] + dvo["attn"] + dc + 1)
                    ctx.inc(sc.activation(xsq[:, dc * 512:(dc + 1) * 512],
                                          attn[:, dc * 512:(dc + 1) * 512],
                                          AF.Square), "ACT")
                sc.wait_ge(ctx.sem["DVE"], dvb[qc] + dvo["rows2"] + 3)
                ctx.inc(sc.activation(r_sq, r_var, AF.Sqrt,
                                      bias=ctx.constf(1e-5)[0:1, :]), "ACT")
                for dc in range(8):
                    sc.wait_ge(ctx.sem["DVE"], dvb[qc] + dvo["apply2"] + 2 * (dc + 1))
                    ctx.inc(sc.add(y2[:, dc * 512:(dc + 1) * 512],
                                   y2[:, dc * 512:(dc + 1) * 512],
                                   lnb[:, dc:dc + 1]), "ACT")

        @block.tensor
        def _(t):
            t.wait_ge(ctx.sem["g3"], c0["g3"] + 16 * 7)
            for qc in range(nqc):
                dvo = dv_off(qc)
                peo = pe_off(qc)
                aco = ac_off(qc)
                t.wait_ge(ctx.sem["DVE"], dvb[qc] + 8)
                t.wait_ge(ctx.sem["ACT"], acb[qc] + 8)
                for dc in range(8):
                    ctx.inc(t.matmul(pS1[0:1, :], ctx.ones_b,
                                     xbf[:, dc * 512:(dc + 1) * 512],
                                     start=(dc == 0), stop=(dc == 7)), "PE")
                    ctx.inc(t.matmul(pS2[0:1, :], ctx.ones_b,
                                     xsq[:, dc * 512:(dc + 1) * 512],
                                     start=(dc == 0), stop=(dc == 7)), "PE")
                t.wait_ge(ctx.sem["DVE"], dvb[qc] + dvo["rows"] + 1)
                ctx.inc(t.matmul(pB1[:], ctx.ones_f, r_mu, start=True, stop=True), "PE")
                t.wait_ge(ctx.sem["DVE"], dvb[qc] + dvo["rows"] + 4)
                ctx.inc(t.matmul(pB2[:], ctx.ones_f, r_r, start=True, stop=True), "PE")
                t.wait_ge(ctx.sem["DVE"], dvb[qc] + dvo["gapply"] + 16)
                for oc in range(8):
                    if qc > 0 or oc >= 2:
                        an = acb[qc] + aco["qevac"] + oc - 1 if oc >= 2 \
                            else acb[qc - 1] + 82
                        t.wait_ge(ctx.sem["ACT"], an)
                    for dc in range(8):
                        ctx.inc(t.matmul(pps[oc % 2][:],
                                         wqkv[:, dc * 3072 + oc * 128:
                                              dc * 3072 + oc * 128 + 128],
                                         gsb[:, dc * 512:(dc + 1) * 512],
                                         start=(dc == 0), stop=(dc == 7)), "PE")
                for oc in range(8):
                    an = acb[qc] + aco["kevac"] + 4 * (oc - 1) if oc >= 2 \
                        else acb[qc] + aco["qevac"] + 8
                    t.wait_ge(ctx.sem["ACT"], an)
                    for dc in range(8):
                        ctx.inc(t.matmul(pps[oc % 2][:],
                                         wqkv[:, dc * 3072 + 1024 + oc * 128:
                                              dc * 3072 + 1024 + oc * 128 + 128],
                                         gsb[:, dc * 512:(dc + 1) * 512],
                                         start=(dc == 0), stop=(dc == 7)), "PE")
                for tt in range(4):
                    for hf in range(2):
                        i2 = 2 * tt + hf
                        an = acb[qc] + aco["vevac"] + i2 - 1 if i2 >= 2 \
                            else acb[qc] + aco["kevac"] + 32
                        t.wait_ge(ctx.sem["ACT"], an)
                        for dc in range(8):
                            ctx.inc(t.matmul(
                                pps[i2 % 2][:],
                                gsb[:, dc * 512 + tt * 128:dc * 512 + tt * 128 + 128],
                                wqkv[:, dc * 3072 + 2048 + hf * 512:
                                     dc * 3072 + 2048 + hf * 512 + 512],
                                start=(dc == 0), stop=(dc == 7)), "PE")
                for j, (r, kt) in enumerate(kts(qc)):
                    if j < 2:
                        t.wait_ge(ctx.sem["ACT"], acb[qc] + aco["vevac"] + 8)
                    else:
                        t.wait_ge(ctx.sem["DVE"], dvb[qc] + dvo["smask"] + j - 1)
                    slot = kt % 6
                    for dc in range(8):
                        ctx.inc(t.matmul(
                            pps[j % 2][:],
                            kring[:, (dc * 6 + slot) * 128:(dc * 6 + slot) * 128 + 128],
                            qsb[:, dc * 512:(dc + 1) * 512],
                            start=(dc == 0), stop=(dc == 7)), "PE")
                for j, (r, kt) in enumerate(kts(qc)):
                    t.wait_ge(ctx.sem["DVE"], dvb[qc] + dvo["smask"] + j + 1)
                    ctx.inc(t.matmul(pB1[0:1, :], ctx.ones_b,
                                     ssb[:, r * 512:(r + 1) * 512],
                                     start=(j == 0), stop=(j == len(kts(qc)) - 1)),
                            "PE")
                t.wait_ge(ctx.sem["DVE"], dvb[qc] + dvo["rsrow"] + 1)
                ctx.inc(t.matmul(pB2[:], ctx.ones_f, rs_row, start=True, stop=True),
                        "PE")
                for dc in range(8):
                    if dc >= 2:
                        t.wait_ge(ctx.sem["ACT"], acb[qc] + aco["oevac"] + dc - 1)
                    elif qc > 0:
                        t.wait_ge(ctx.sem["ACT"],
                                  acb[qc - 1] + ac_off(0)["oevac"] + dc + 7)
                    for j, (r, kt) in enumerate(kts(qc)):
                        slot = kt % 6
                        ctx.inc(t.matmul(
                            pos[dc % 2][:],
                            vring[:, slot * 1024 + dc * 128:slot * 1024 + dc * 128 + 128],
                            ssb[:, r * 512:(r + 1) * 512],
                            start=(j == 0), stop=(j == len(kts(qc)) - 1)), "PE")
                t.wait_ge(ctx.sem["DVE"], dvb[qc] + dvo["xbf2"] + 8)
                t.wait_ge(ctx.sem["ACT"], acb[qc] + aco["sq2"] + 8)
                for dc in range(8):
                    ctx.inc(t.matmul(pS1[0:1, :], ctx.ones_b,
                                     xbf[:, dc * 512:(dc + 1) * 512],
                                     start=(dc == 0), stop=(dc == 7)), "PE")
                    ctx.inc(t.matmul(pS2[0:1, :], ctx.ones_b,
                                     xsq[:, dc * 512:(dc + 1) * 512],
                                     start=(dc == 0), stop=(dc == 7)), "PE")
                t.wait_ge(ctx.sem["DVE"], dvb[qc] + dvo["rows2"] + 1)
                ctx.inc(t.matmul(pB1[:], ctx.ones_f, r_mu, start=True, stop=True), "PE")
                t.wait_ge(ctx.sem["DVE"], dvb[qc] + dvo["rows2"] + 4)
                ctx.inc(t.matmul(pB2[:], ctx.ones_f, r_r, start=True, stop=True), "PE")



def p_l0ffn(ctx, tag, parts="ABC"):
    nc, T, ap = ctx.nc, ctx.T, ctx.ap
    nqc = T // 512
    QW = min(1024, T)
    nquart = T // QW
    NHF = QW // 512

    def KT(qc):
        return 4 * (qc + 1)

    with contextlib.ExitStack() as st0:
        def sb0(name, shape, dt):
            return st0.enter_context(nc.sbuf_tensor(tag + name, shape, dt))
        pair = sb0("_pair", [128, 2], BF16)
        krow = sb0("_krow", [1, 16], F32)
        kfs = sb0("_kfs", [2, 4], F32)
        rotP = sb0("_rotP", [128, 128], BF16)
        ps = [st0.enter_context(nc.psum_tensor(tag + "_ps%d" % i, [128, 512], F32))
              for i in range(8)]
        stKV = st0.enter_context(contextlib.ExitStack())
        ksb = stKV.enter_context(nc.sbuf_tensor(tag + "_k", [128, 4 * T], BF16))
        vsb = stKV.enter_context(
            nc.sbuf_tensor(tag + "_v", [128, (T // 128) * 520], BF16))

        with contextlib.ExitStack() as stA:
            def sbA(name, shape, dt):
                return stA.enter_context(nc.sbuf_tensor(tag + "A" + name, shape, dt))
            XtA0 = sbA("_Xt0", [128, 8 * 512], F32)
            XtA1 = sbA("_Xt1", [128, 8 * 512], F32)
            xbf = sbA("_xbf", [128, 8 * 512], BF16)
            xsq = sbA("_xsq", [128, 8 * 512], BF16)
            hsb = sbA("_h", [128, 8 * 512], BF16)
            kraw = sbA("_kraw", [128, 4 * 512], BF16)
            ksq0 = sbA("_ksq0", [128, 512], BF16)
            ksq1 = sbA("_ksq1", [128, 512], BF16)
            tA = sbA("_tA", [128, 512], BF16)
            tB = sbA("_tB", [128, 512], BF16)
            tc0 = sbA("_tc0", [128, 512], F32)
            tc1 = sbA("_tc1", [128, 512], F32)
            wk = sbA("_wk", [128, 8 * 512], BF16)
            wv = sbA("_wv", [128, 8 * 512], BF16)
            ck = sbA("_ck", [128, T], BF16)
            sk = sbA("_sk", [128, T], BF16)
            ln1 = sbA("_ln1", [128, 8], F32)
            rows = sbA("_rows", [1, 9 * 512], F32)
            kn2m = sbA("_kn2m", [2, 4 * nqc], F32)
            kfin = sbA("_kfin", [2, 4], F32)
            r_mu = rows[:, 0:512]; r_m2 = rows[:, 512:1024]
            r_var = rows[:, 1024:1536]; r_r = rows[:, 1536:2048]
            r_sq = rows[:, 2048:2560]; r_rr = rows[:, 2560:3072]
            r_my = rows[:, 3072:3584]; r_sq2 = rows[:, 3584:4096]
            r_r2 = rows[:, 512:1024]; r_c = rows[:, 4096:4608]
            XtAs = [XtA0, XtA1]

            def sA_sync(s):
                for tc in range(nqc):
                    if tc >= 2:
                        ctx.need(s, (tag, "A_xtfree", tc - 2))
                    i = ap["X"][:, :, tc * 512:(tc + 1) * 512] \
                        .rearrange("c p n -> p c n")
                    ctx.inc(s.dma_start(
                        XtAs[tc % 2][:].rearrange("p (c n) -> p c n", n=512), i),
                        ["h0", "h1"][tc % 2], 16)
                    ctx.mark((tag, "A_xl", tc), ["h0", "h1"][tc % 2])

            def sA_gpsimd(g):
                ctx.inc(g.memset(pair[0:64, 0:1], 1.0), "POOL")
                ctx.inc(g.memset(pair[0:64, 1:2], 0.0), "POOL")
                ctx.inc(g.memset(pair[64:128, 0:1], 0.0), "POOL")
                ctx.inc(g.memset(pair[64:128, 1:2], 1.0), "POOL")
                ctx.inc(g.memset(vsb[:], 1.0), "POOL")
                ctx.mark((tag, "A_vones"), "POOL")
                pre = ("p0", "attnw") in ctx.sched
                if pre:
                    ctx.need(g, ("p0", "attnw"))
                for vap, dst in ((ap["L0_ffn_ln1_w"], ln1),):
                    ctx.inc(g.dma_start(dst[:], vap.rearrange("(c p) -> p c", p=128),
                                        allow_slow_non_contiguous=True), "g3", 16)
                ctx.inc(g.dma_start(rotP[:], ap["c_rotP"]), "g3", 16)
                wksrc = ap["s_wkc"] if pre else ap["L0_ffn_wk"]
                for c in range(4):
                    for b in range(2):
                        kvh = c + 4 * b
                        dst = bass.AP(wk, c * 128 + b * 64,
                                      [[8 * 512, 128], [512, 8], [1, 64]])
                        src = wksrc[:, kvh * 64:(kvh + 1) * 64] \
                            .rearrange("(z p) n -> p z n", p=128)
                        ctx.inc(g.dma_start(dst, src), "g3", 16)
                wvsrc = ap["s_wvc"] if pre else ap["L0_ffn_wv"]
                ctx.inc(g.dma_start(wv[:].rearrange("p (z n) -> p z n", n=512),
                                    wvsrc.rearrange("(z p) n -> p z n", p=128)),
                        "g3", 16)
                ctx.inc(g.dma_start(ck[:], ap["c_ropeCk"]), "g3", 16)
                ctx.inc(g.dma_start(sk[:], ap["c_ropeSk"]), "g3", 16)
                ctx.mark((tag, "A_const"), "g3")

            def sA_vector(v):
                def a_copies(tc):
                    ctx.need(v, (tag, "A_xl", tc))
                    if tc >= 1:
                        ctx.need(v, (tag, "A_sm", tc - 1))
                    for dc in range(8):
                        ctx.inc(v.tensor_copy(xbf[:, dc * 512:(dc + 1) * 512],
                                              XtAs[tc % 2][:, dc * 512:(dc + 1) * 512]),
                                "DVE")
                    ctx.mark((tag, "A_cp", tc), "DVE")
                a_copies(0)
                for tc in range(nqc):
                    ctx.need(v, (tag, "A_sm", tc))
                    ctx.inc(v.tensor_scalar_mul(r_mu, ps[0][0:1, :], 1.0 / 1024), "DVE")
                    ctx.fence(v, "DVE")
                    ctx.inc(v.tensor_tensor(r_m2, r_mu, r_mu, ALU.mult), "DVE")
                    ctx.fence(v, "DVE")
                    ctx.inc(v.scalar_tensor_tensor(r_var, ps[1][0:1, :], 1.0 / 1024,
                                                   r_m2, ALU.mult, ALU.subtract), "DVE")
                    ctx.mark((tag, "A_var", tc), "DVE")
                    ctx.need(v, (tag, "A_sqrt1", tc))
                    ctx.inc(v.reciprocal(r_r, r_sq), "DVE")
                    ctx.fence(v, "DVE")
                    ctx.inc(v.tensor_tensor(r_rr, r_r, r_r, ALU.mult), "DVE")
                    ctx.fence(v, "DVE")
                    ctx.inc(v.tensor_tensor(r_my, r_var, r_rr, ALU.mult), "DVE")
                    ctx.mark((tag, "A_my", tc), "DVE")
                    ctx.need(v, (tag, "A_sqrt2", tc))
                    ctx.inc(v.reciprocal(r_r2, r_sq2), "DVE")
                    ctx.fence(v, "DVE")
                    ctx.inc(v.tensor_tensor(r_c, r_r, r_r2, ALU.mult), "DVE")
                    ctx.mark((tag, "A_rc", tc), "DVE")
                    ctx.need(v, (tag, "A_bc", tc))
                    for dc in range(8):
                        tcb = [tc0, tc1][dc % 2]
                        if dc >= 2:
                            ctx.fence(v, "DVE")
                        ctx.inc(v.scalar_tensor_tensor(
                            tcb[:], XtAs[tc % 2][:, dc * 512:(dc + 1) * 512], 1.0,
                            ps[6][:], ALU.mult, ALU.subtract), "DVE")
                        ctx.fence(v, "DVE")
                        ctx.inc(v.scalar_tensor_tensor(
                            hsb[:, dc * 512:(dc + 1) * 512], tcb[:], ln1[:, dc:dc + 1],
                            ps[7][:], ALU.mult, ALU.mult), "DVE")
                    ctx.mark((tag, "A_h", tc), "DVE")
                    ctx.mark((tag, "A_xtfree", tc), "DVE")
                    if tc + 1 < nqc:
                        a_copies(tc + 1)
                    for c in range(4):
                        ctx.need(v, (tag, "A_kevac", tc, c))
                        ctx.fence(v, "DVE")
                        ctx.inc(v.tensor_tensor(tA[:], kraw[:, c * 512:(c + 1) * 512],
                                                ck[:, tc * 512:(tc + 1) * 512],
                                                ALU.mult), "DVE")
                        ctx.need(v, (tag, "A_rot", tc, c))
                        ctx.inc(v.tensor_tensor(tB[:], ps[2 + c % 2][:],
                                                sk[:, tc * 512:(tc + 1) * 512],
                                                ALU.mult), "DVE")
                        ctx.fence(v, "DVE")
                        ctx.inc(v.tensor_tensor(
                            ksb[:, c * T + tc * 512:c * T + (tc + 1) * 512],
                            tA[:], tB[:], ALU.add), "DVE")
                        ctx.mark((tag, "A_krope", tc, c), "DVE")
                    for c in range(4):
                        ctx.need(v, (tag, "A_pair", tc, c))
                        ctx.inc(v.reduce_max(kn2m[:, c * nqc + tc:c * nqc + tc + 1],
                                             ps[6 + c % 2][0:2, :], axis=AX), "DVE")
                        ctx.mark((tag, "A_kn2", tc, c), "DVE")
                if True:
                    for c in range(4):
                        ctx.fence(v, "DVE")
                        ctx.inc(v.reduce_max(kfin[:, c:c + 1],
                                             kn2m[:, c * nqc:(c + 1) * nqc], axis=AX),
                                "DVE")
                    ctx.mark((tag, "A_kfin"), "DVE")

            def sA_scalar(sc):
                def a_squares(tc):
                    ctx.need(sc, (tag, "A_xl", tc))
                    if tc >= 1:
                        ctx.need(sc, (tag, "A_sm", tc - 1))
                    for dc in range(8):
                        ctx.inc(sc.activation(xsq[:, dc * 512:(dc + 1) * 512],
                                              XtAs[tc % 2][:, dc * 512:(dc + 1) * 512],
                                              AF.Square), "ACT")
                    ctx.mark((tag, "A_sq", tc), "ACT")
                a_squares(0)
                for tc in range(nqc):
                    ctx.need(sc, (tag, "A_var", tc))
                    ctx.inc(sc.activation(r_sq, r_var, AF.Sqrt,
                                          bias=ctx.constf(1e-5)[0:1, :]), "ACT")
                    ctx.mark((tag, "A_sqrt1", tc), "ACT")
                    ctx.need(sc, (tag, "A_my", tc))
                    ctx.inc(sc.activation(r_sq2, r_my, AF.Sqrt,
                                          bias=ctx.constf(1e-6)[0:1, :]), "ACT")
                    ctx.mark((tag, "A_sqrt2", tc), "ACT")
                    for c in range(4):
                        ctx.need(sc, (tag, "A_kp", tc, c))
                        ctx.inc(sc.copy(kraw[:, c * 512:(c + 1) * 512],
                                        ps[c % 2][:]), "ACT")
                        ctx.mark((tag, "A_kevac", tc, c), "ACT")
                    if tc + 1 < nqc:
                        a_squares(tc + 1)
                    for c in range(4):
                        ctx.need(sc, (tag, "A_krope", tc, c))
                        ksq = [ksq0, ksq1][c % 2]
                        if c >= 2:
                            ctx.need(sc, (tag, "A_pair", tc, c - 2), delta=0)
                        ctx.inc(sc.activation(
                            ksq[:], ksb[:, c * T + tc * 512:c * T + (tc + 1) * 512],
                            AF.Square), "ACT")
                        ctx.mark((tag, "A_ksq", tc, c), "ACT")
                    for tt in range(4):
                        ctx.need(sc, (tag, "A_vp", tc, tt))
                        gt = tc * 4 + tt
                        dst = bass.AP(vsb, gt * 520, [[(T // 128) * 520, 128],
                                                      [65, 8], [1, 64]])
                        ctx.inc(sc.copy(dst, ps[4 + tt % 2][:]
                                        .rearrange("p (k n) -> p k n", n=64)), "ACT")
                        ctx.mark((tag, "A_vevac", tc, tt), "ACT")
                ctx.need(sc, (tag, "A_kfin"))
                ctx.inc(sc.activation(kfs[:], kfin[:], AF.Sqrt), "ACT")
                ctx.mark((tag, "A_kfs"), "ACT")

            def sA_tensor(t):
                ctx.need(t, (tag, "A_const"))
                ctx.need(t, (tag, "A_vones"))
                for tc in range(nqc):
                    ctx.need(t, (tag, "A_cp", tc))
                    ctx.need(t, (tag, "A_sq", tc))
                    if tc >= 1:
                        ctx.need(t, (tag, "A_var", tc - 1))
                        ctx.need(t, (tag, "A_kevac", tc - 1, 3))
                    for dc in range(8):
                        ctx.inc(t.matmul(ps[0][0:1, :], ctx.ones_b,
                                         xbf[:, dc * 512:(dc + 1) * 512],
                                         start=(dc == 0), stop=(dc == 7)), "PE")
                        ctx.inc(t.matmul(ps[1][0:1, :], ctx.ones_b,
                                         xsq[:, dc * 512:(dc + 1) * 512],
                                         start=(dc == 0), stop=(dc == 7)), "PE")
                    ctx.mark((tag, "A_sm", tc), "PE")
                    ctx.need(t, (tag, "A_var", tc))
                    if tc >= 1:
                        ctx.need(t, (tag, "A_h", tc - 1))
                        ctx.need(t, (tag, "A_kn2", tc - 1, 3))
                    ctx.inc(t.matmul(ps[6][:], ctx.ones_f, r_mu,
                                     start=True, stop=True), "PE")
                    ctx.need(t, (tag, "A_rc", tc))
                    ctx.inc(t.matmul(ps[7][:], ctx.ones_f, r_c,
                                     start=True, stop=True), "PE")
                    ctx.mark((tag, "A_bc", tc), "PE")
                    ctx.need(t, (tag, "A_h", tc))
                    for c in range(4):
                        if c >= 2:
                            ctx.need(t, (tag, "A_kevac", tc, c - 2))
                        elif tc >= 1:
                            ctx.need(t, (tag, "A_kevac", tc - 1, c + 2))
                        for dc in range(8):
                            ctx.inc(t.matmul(ps[c % 2][:],
                                             wk[:, dc * 512 + c * 128:
                                                dc * 512 + (c + 1) * 128],
                                             hsb[:, dc * 512:(dc + 1) * 512],
                                             start=(dc == 0), stop=(dc == 7)), "PE")
                        ctx.mark((tag, "A_kp", tc, c), "PE")
                    for c in range(4):
                        ctx.need(t, (tag, "A_kevac", tc, c))
                        if c >= 2:
                            ctx.need(t, (tag, "A_krope", tc, c - 2))
                        elif tc >= 1:
                            ctx.need(t, (tag, "A_krope", tc - 1, c + 2))
                        ctx.inc(t.matmul(ps[2 + c % 2][:], rotP[:],
                                         kraw[:, c * 512:(c + 1) * 512],
                                         start=True, stop=True), "PE")
                        ctx.mark((tag, "A_rot", tc, c), "PE")
                    ctx.need(t, (tag, "A_h", tc))
                    for c in range(4):
                        ctx.need(t, (tag, "A_ksq", tc, c))
                        if c >= 2:
                            ctx.need(t, (tag, "A_kn2", tc, c - 2))
                        ksq = [ksq0, ksq1][c % 2]
                        ctx.inc(t.matmul(ps[6 + c % 2][0:2, :], pair[:], ksq[:],
                                         start=True, stop=True), "PE")
                        ctx.mark((tag, "A_pair", tc, c), "PE")
                    for tt in range(4):
                        if tt >= 2:
                            ctx.need(t, (tag, "A_vevac", tc, tt - 2))
                        elif tc >= 1:
                            ctx.need(t, (tag, "A_vevac", tc - 1, tt + 2))
                        for dc in range(8):
                            ctx.inc(t.matmul(
                                ps[4 + tt % 2][:],
                                hsb[:, dc * 512 + tt * 128:dc * 512 + tt * 128 + 128],
                                wv[:, dc * 512:(dc + 1) * 512],
                                start=(dc == 0), stop=(dc == 7)), "PE")
                        ctx.mark((tag, "A_vp", tc, tt), "PE")

            if "A" in parts:
                run_two_pass(ctx, [("sync", sA_sync), ("gpsimd", sA_gpsimd),
                                   ("vector", sA_vector), ("scalar", sA_scalar),
                                   ("tensor", sA_tensor)])
        barrier(ctx, tag + "A")

        with contextlib.ExitStack() as stB:
            def sbB(name, shape, dt):
                return stB.enter_context(nc.sbuf_tensor(tag + "B" + name, shape, dt))
            XtBs = [sbB("_Xt%d" % i, [128, 8 * 512], F32)
                    for i in range(2)]
            hq = sbB("_h", [128, 8 * 512], BF16)
            qhat = sbB("_qh", [128, 8 * 512], BF16)
            xbf = hq
            xsq = qhat
            qraw0 = sbB("_qr0", [128, 512], BF16)
            qraw1 = sbB("_qr1", [128, 512], BF16)
            qsq0 = sbB("_qs0", [128, 512], BF16)
            qsq1 = sbB("_qs1", [128, 512], BF16)
            tA = sbB("_tA", [128, 512], BF16)
            tB = sbB("_tB", [128, 512], BF16)
            tf0 = sbB("_tf0", [128, 512], F32)
            tf1 = sbB("_tf1", [128, 512], F32)
            cq = sbB("_cq", [128, 512], BF16)
            sq = sbB("_sq", [128, 512], BF16)
            rows = sbB("_rows", [1, 9 * 512], F32)
            qn2m = sbB("_qn2m", [2, 8], F32)
            mrow = sbB("_mrow", [1, 16], F32)
            mpart = sbB("_mpart", [1, 16], F32)
            msq = sbB("_msq", [1, 16], F32)
            mneg = sbB("_mneg", [1, 16], F32)
            mbs = sbB("_mb", [128, 16], F32)
            rbn = sbB("_rbn", [1, 2 * 512], F32)
            PA = [sbB("_PA%d" % i, [128, 512], BF16) for i in range(3)]
            PB = [sbB("_PB%d" % i, [128, 512], BF16) for i in range(3)]
            oA = sbB("_oA", [64, 512], F32)
            oB = sbB("_oB", [64, 512], F32)
            ao = sbB("_ao", [128, 8 * 512], BF16)
            wq = sbB("_wq", [128, 8 * 1024], BF16)
            wo = sbB("_wo", [128, 8 * 1024], BF16)
            ln1b = sbB("_ln1", [128, 8], F32)
            caus = sbB("_caus", [128, 4 * 512], BF16)
            r_mu = rows[:, 0:512]; r_m2 = rows[:, 512:1024]
            r_var = rows[:, 1024:1536]; r_r = rows[:, 1536:2048]
            r_sq = rows[:, 2048:2560]; r_rr = rows[:, 2560:3072]
            r_my = rows[:, 3072:3584]; r_sq2 = rows[:, 3584:4096]
            r_r2 = rows[:, 512:1024]; r_c = rows[:, 4096:4608]

            with nc.Block() as blkW:
                @blkW.gpsimd
                def _(g):
                    pre = ("p0", "attnw") in ctx.sched
                    if pre:
                        ctx.need(g, ("p0", "attnw"))
                    wqsrc = ap["s_wqc"] if pre else ap["L0_ffn_wq"]
                    wosrc = ap["s_woc"] if pre else ap["L0_ffn_wo"]
                    for a in range(8):
                        c, r = a // 2, a % 2
                        for b in range(2):
                            h = 2 * c + r + 8 * b
                            dst = bass.AP(wq, a * 128 + b * 64,
                                          [[8 * 1024, 128], [1024, 8], [1, 64]])
                            src = wqsrc[:, h * 64:(h + 1) * 64] \
                                .rearrange("(z p) n -> p z n", p=128)
                            ctx.inc(g.dma_start(dst, src), "g3", 16)
                            dstw = wo[b * 64:(b + 1) * 64, a * 1024:(a + 1) * 1024]
                            ctx.inc(g.dma_start(dstw, wosrc[h * 64:(h + 1) * 64, :]),
                                    "g3", 16)
                    ctx.inc(g.dma_start(caus[:].rearrange("p (j n) -> p j n", n=512),
                                        ap["c_caus4"].rearrange("j p n -> p j n")),
                            "g3", 16)
                    ctx.inc(g.dma_start(ln1b[:], ap["L0_ffn_ln1_w"]
                                        .rearrange("(c p) -> p c", p=128),
                                        allow_slow_non_contiguous=True), "g3", 16)
                    ctx.sched[(tag, "B_const")] = ("g3", ctx.c["g3"])
                    if getattr(ctx, "cast_moe_at", None) == tag:
                        for e in range(8):
                            for csrc, cdst in (("L0_ffn_w_gate", "s_g"),
                                               ("L0_ffn_w_up", "s_u"),
                                               ("L0_ffn_w_down", "s_d")):
                                ctx.inc(g.dma_start(ap[cdst][e], ap[csrc][e]),
                                        "gC", 16)
                        ctx.sched[("p0", "moe")] = ("gC", ctx.c["gC"])
                    ctx.need(g, (tag, "A_kfs"))
                    ctx.inc(g.dma_start(ap["s_small"][0:1, 0:8], kfs[:]), "g2", 16)
                    g.wait_ge(ctx.sem["g2"], ctx.c["g2"])
                    for c in range(4):
                        for rep in range(2):
                            srcap = bass.AP(ap["s_small"].tensor, c,
                                            [[4096, 1], [4, 2]])
                            ctx.inc(g.dma_start(
                                krow[:, 4 * c + 2 * rep:4 * c + 2 * rep + 2],
                                srcap, allow_slow_non_contiguous=True), "g2", 16)
                    g.wait_ge(ctx.sem["g2"], ctx.c["g2"])
                    ctx.sched[(tag, "A_krow")] = ("g2", ctx.c["g2"])

            def diag(qc, kt):
                return kt >= 4 * qc

            def sB_sync(s):
                for it in range(nqc + 1):
                    if it < nqc:
                        qc = it
                        if qc >= 1:
                            ctx.need(s, (tag, "B_qrope", qc - 1, 7))
                        ctx.inc(s.dma_start(cq[:],
                                            ap["c_ropeCq"][:, qc * 512:(qc + 1) * 512]),
                                "h4", 16)
                        ctx.inc(s.dma_start(sq[:],
                                            ap["c_ropeSq"][:, qc * 512:(qc + 1) * 512]),
                                "h4", 16)
                        ctx.mark((tag, "B_rope_tab", qc), "h4")
                        if qc >= 2:
                            ctx.need(s, (tag, "B_store", qc - 2))
                        i = ap["X"][:, :, qc * 512:(qc + 1) * 512] \
                            .rearrange("c p n -> p c n")
                        ctx.inc(s.dma_start(
                            XtBs[qc % 2][:].rearrange("p (c n) -> p c n", n=512), i),
                            ["h2", "h3"][qc % 2], 16)
                        ctx.mark((tag, "B_xl", qc), ["h2", "h3"][qc % 2])
                    if it >= 1:
                        qc = it - 1
                        ctx.need(s, (tag, "B_qn2", qc))
                        ctx.inc(s.dma_start(ap["s_small"][0:1, 16:32], qn2m[:]),
                                "h5", 16)
                        ctx.w(s, "h5")
                        ctx.inc(s.dma_start(mpart[:], ap["s_small"][0:1, 16:32]),
                                "h5", 16)
                        ctx.w(s, "h5")
                        ctx.mark((tag, "B_mrow", qc), "h5")
                        ctx.need(s, (tag, "B_x1", qc))
                        o = ap["X1"][:, :, qc * 512:(qc + 1) * 512] \
                            .rearrange("c p n -> p c n")
                        ctx.inc(s.dma_start(
                            o, XtBs[qc % 2][:].rearrange("p (c n) -> p c n", n=512)),
                            ["h6", "h7"][qc % 2], 16)
                        ctx.mark((tag, "B_store", qc), ["h6", "h7"][qc % 2])

            def sB_gpsimd(g):
                pass

            def sB_vector(v):
                for qc in range(nqc):
                    ctx.need(v, (tag, "B_xl", qc))
                    if qc >= 1:
                        ctx.need(v, (tag, "B_sm", qc - 1))
                    for dc in range(8):
                        ctx.inc(v.tensor_copy(xbf[:, dc * 512:(dc + 1) * 512],
                                              XtBs[qc % 2][:, dc * 512:(dc + 1) * 512]), "DVE")
                    ctx.mark((tag, "B_cp", qc), "DVE")
                    ctx.need(v, (tag, "B_sm", qc))
                    ctx.inc(v.tensor_scalar_mul(r_mu, ps[0][0:1, :], 1.0 / 1024), "DVE")
                    ctx.fence(v, "DVE")
                    ctx.inc(v.tensor_tensor(r_m2, r_mu, r_mu, ALU.mult), "DVE")
                    ctx.fence(v, "DVE")
                    ctx.inc(v.scalar_tensor_tensor(r_var, ps[1][0:1, :], 1.0 / 1024,
                                                   r_m2, ALU.mult, ALU.subtract), "DVE")
                    ctx.mark((tag, "B_var", qc), "DVE")
                    ctx.need(v, (tag, "B_sqrt1", qc))
                    ctx.inc(v.reciprocal(r_r, r_sq), "DVE")
                    ctx.fence(v, "DVE")
                    ctx.inc(v.tensor_tensor(r_rr, r_r, r_r, ALU.mult), "DVE")
                    ctx.fence(v, "DVE")
                    ctx.inc(v.tensor_tensor(r_my, r_var, r_rr, ALU.mult), "DVE")
                    ctx.mark((tag, "B_my", qc), "DVE")
                    ctx.need(v, (tag, "B_sqrt2", qc))
                    ctx.inc(v.reciprocal(r_r2, r_sq2), "DVE")
                    ctx.fence(v, "DVE")
                    ctx.inc(v.tensor_tensor(r_c, r_r, r_r2, ALU.mult), "DVE")
                    ctx.mark((tag, "B_rc", qc), "DVE")
                    ctx.need(v, (tag, "B_bc", qc))
                    for dc in range(8):
                        tcb = [tf0, tf1][dc % 2]
                        if dc >= 2:
                            ctx.fence(v, "DVE")
                        ctx.inc(v.scalar_tensor_tensor(
                            tcb[:], XtBs[qc % 2][:, dc * 512:(dc + 1) * 512], 1.0,
                            ps[6][:], ALU.mult, ALU.subtract), "DVE")
                        ctx.fence(v, "DVE")
                        ctx.inc(v.scalar_tensor_tensor(
                            hq[:, dc * 512:(dc + 1) * 512], tcb[:], ln1b[:, dc:dc + 1],
                            ps[7][:], ALU.mult, ALU.mult), "DVE")
                    ctx.mark((tag, "B_h", qc), "DVE")
                    for a in range(8):
                        ctx.need(v, (tag, "B_qevac", qc, a))
                        if a == 0:
                            ctx.need(v, (tag, "B_rope_tab", qc))
                        ctx.fence(v, "DVE")
                        ctx.inc(v.tensor_tensor(tA[:], [qraw0, qraw1][a % 2][:],
                                                cq[:], ALU.mult), "DVE")
                        ctx.need(v, (tag, "B_rot", qc, a))
                        ctx.inc(v.tensor_tensor(tB[:], ps[2 + a % 2][:],
                                                sq[:], ALU.mult), "DVE")
                        ctx.fence(v, "DVE")
                        ctx.inc(v.tensor_tensor(qhat[:, a * 512:(a + 1) * 512],
                                                tA[:], tB[:], ALU.add), "DVE")
                        ctx.mark((tag, "B_qrope", qc, a), "DVE")
                    for a in range(8):
                        ctx.need(v, (tag, "B_pair", qc, a))
                        ctx.inc(v.reduce_max(qn2m[:, a:a + 1], ps[7][0:2, :],
                                             axis=AX), "DVE")
                        ctx.mark((tag, "B_qn2r", qc, a), "DVE")
                    if ctx.recording:
                        ctx.sched[(tag, "B_qn2", qc)] = ctx.sched[(tag, "B_qn2r", qc, 7)]
                    ctx.need(v, (tag, "B_mrow", qc))
                    mpv = bass.AP(mpart, 0, [[16, 1], [1, 8], [8, 2]])
                    ctx.fence(v, "DVE")
                    ctx.inc(v.tensor_tensor(mrow[:].rearrange("o (a b) -> o a b", b=2),
                                            mpv, mpv, ALU.max), "DVE")
                    ctx.mark((tag, "B_mrowx", qc), "DVE")
                    ctx.need(v, (tag, "B_msq", qc))
                    ctx.fence(v, "DVE")
                    ctx.inc(v.scalar_tensor_tensor(mneg[:], msq[:], -1.0, krow[:],
                                                   ALU.mult, ALU.mult), "DVE")
                    ctx.mark((tag, "B_mneg", qc), "DVE")
                    for a in range(8):
                        nkt = KT(qc)
                        for kt in range(nkt):
                            if diag(qc, kt):
                                j = kt - 4 * qc
                                ctx.need(v, (tag, "B_expA", qc, a, kt))
                                ctx.inc(v.tensor_tensor(PA[kt % 3][:], PA[kt % 3][:],
                                                        caus[:, j * 512:(j + 1) * 512],
                                                        ALU.mult), "DVE")
                                ctx.mark((tag, "B_mA", qc, a, kt), "DVE")
                                ctx.need(v, (tag, "B_expB", qc, a, kt))
                                ctx.inc(v.tensor_tensor(PB[kt % 3][:], PB[kt % 3][:],
                                                        caus[:, j * 512:(j + 1) * 512],
                                                        ALU.mult), "DVE")
                                ctx.mark((tag, "B_mB", qc, a, kt), "DVE")
                        ctx.need(v, (tag, "B_pvA", qc, a))
                        ctx.inc(v.reciprocal(rbn[:, 0:512], ps[4][64:65, :]), "DVE")
                        ctx.mark((tag, "B_rbnA", qc, a), "DVE")
                        ctx.need(v, (tag, "B_bnA", qc, a))
                        ctx.need(v, (tag, "B_oevA", qc, a))
                        ctx.inc(v.tensor_tensor(ao[0:64, a * 512:(a + 1) * 512],
                                                oA[:], ps[6][0:64, :], ALU.mult), "DVE")
                        ctx.mark((tag, "B_aoA", qc, a), "DVE")
                        ctx.need(v, (tag, "B_pvB", qc, a))
                        ctx.inc(v.reciprocal(rbn[:, 512:1024], ps[5][64:65, :]), "DVE")
                        ctx.mark((tag, "B_rbnB", qc, a), "DVE")
                        ctx.need(v, (tag, "B_bnB", qc, a))
                        ctx.need(v, (tag, "B_oevB", qc, a))
                        ctx.inc(v.tensor_tensor(ao[64:128, a * 512:(a + 1) * 512],
                                                oB[:], ps[6][64:128, :], ALU.mult),
                                "DVE")
                        ctx.mark((tag, "B_aoB", qc, a), "DVE")
                    for dc in range(8):
                        ctx.need(v, (tag, "B_ybc", qc))
                        ctx.need(v, (tag, "B_wo", qc, dc))
                        tcb = [tf0, tf1][dc % 2]
                        ctx.fence(v, "DVE")
                        ctx.inc(v.scalar_tensor_tensor(
                            tcb[:], XtBs[qc % 2][:, dc * 512:(dc + 1) * 512], 1.0,
                            ps[6][:], ALU.mult, ALU.subtract), "DVE")
                        ctx.fence(v, "DVE")
                        ctx.inc(v.scalar_tensor_tensor(
                            tcb[:], tcb[:], 1.0, ps[7][:], ALU.mult, ALU.mult), "DVE")
                        ctx.fence(v, "DVE")
                        ctx.inc(v.tensor_tensor(XtBs[qc % 2][:, dc * 512:(dc + 1) * 512],
                                                tcb[:], ps[dc % 2][:], ALU.add), "DVE")
                        ctx.mark((tag, "B_x1p", qc, dc), "DVE")
                    ctx.mark((tag, "B_x1", qc), "DVE")

            def sB_scalar(sc):
                for qc in range(nqc):
                    ctx.need(sc, (tag, "B_xl", qc))
                    if qc >= 1:
                        ctx.need(sc, (tag, "B_sm", qc - 1))
                    for dc in range(8):
                        ctx.inc(sc.activation(xsq[:, dc * 512:(dc + 1) * 512],
                                              XtBs[qc % 2][:, dc * 512:(dc + 1) * 512],
                                              AF.Square), "ACT")
                    ctx.mark((tag, "B_sq", qc), "ACT")
                    ctx.need(sc, (tag, "B_var", qc))
                    ctx.inc(sc.activation(r_sq, r_var, AF.Sqrt,
                                          bias=ctx.constf(1e-5)[0:1, :]), "ACT")
                    ctx.mark((tag, "B_sqrt1", qc), "ACT")
                    ctx.need(sc, (tag, "B_my", qc))
                    ctx.inc(sc.activation(r_sq2, r_my, AF.Sqrt,
                                          bias=ctx.constf(1e-6)[0:1, :]), "ACT")
                    ctx.mark((tag, "B_sqrt2", qc), "ACT")
                    for a in range(8):
                        ctx.need(sc, (tag, "B_qp", qc, a))
                        if a >= 2:
                            ctx.need(sc, (tag, "B_qrope", qc, a - 2))
                        ctx.inc(sc.copy([qraw0, qraw1][a % 2][:], ps[a % 2][:]), "ACT")
                        ctx.mark((tag, "B_qevac", qc, a), "ACT")
                    for a in range(8):
                        ctx.need(sc, (tag, "B_qrope", qc, a))
                        if a >= 2:
                            ctx.need(sc, (tag, "B_pair", qc, a - 2))
                        ctx.inc(sc.activation([qsq0, qsq1][a % 2][:],
                                              qhat[:, a * 512:(a + 1) * 512],
                                              AF.Square), "ACT")
                        ctx.mark((tag, "B_qsq", qc, a), "ACT")
                    ctx.need(sc, (tag, "B_mrowx", qc))
                    ctx.inc(sc.activation(msq[:], mrow[:], AF.Sqrt), "ACT")
                    ctx.mark((tag, "B_msq", qc), "ACT")
                    ctx.need(sc, (tag, "B_mb", qc))
                    ctx.inc(sc.copy(mbs[:], ps[6][:, 0:16]), "ACT")
                    ctx.fence(sc, "ACT")
                    ctx.mark((tag, "B_mbs", qc), "ACT")
                    for a in range(8):
                        nkt = KT(qc)
                        for kt in range(nkt):
                            ctx.need(sc, (tag, "B_sA", qc, a, kt))
                            if kt >= 3:
                                ctx.need(sc, (tag, "B_pvAk", qc, a, kt - 3))
                            elif a >= 1:
                                ctx.need(sc, (tag, "B_pvA", qc, a - 1))
                            elif qc >= 1:
                                ctx.need(sc, (tag, "B_pvA", qc - 1, 7))
                            ctx.inc(sc.activation(PA[kt % 3][:], ps[2 + kt % 2][:],
                                                  AF.Exp,
                                                  bias=mbs[:, 2 * a:2 * a + 1]), "ACT")
                            ctx.mark((tag, "B_expA", qc, a, kt), "ACT")
                            ctx.need(sc, (tag, "B_sB", qc, a, kt))
                            if kt >= 3:
                                ctx.need(sc, (tag, "B_pvBk", qc, a, kt - 3))
                            elif a >= 1:
                                ctx.need(sc, (tag, "B_pvB", qc, a - 1))
                            elif qc >= 1:
                                ctx.need(sc, (tag, "B_pvB", qc - 1, 7))
                            ctx.inc(sc.activation(PB[kt % 3][:], ps[0 + kt % 2][:],
                                                  AF.Exp,
                                                  bias=mbs[:, 2 * a + 1:2 * a + 2]),
                                    "ACT")
                            ctx.mark((tag, "B_expB", qc, a, kt), "ACT")
                        ctx.need(sc, (tag, "B_pvA", qc, a))
                        if a >= 1:
                            ctx.need(sc, (tag, "B_aoA", qc, a - 1))
                        elif qc >= 1:
                            ctx.need(sc, (tag, "B_aoA", qc - 1, 7))
                        ctx.inc(sc.copy(oA[:], ps[4][0:64, :]), "ACT")
                        ctx.mark((tag, "B_oevA", qc, a), "ACT")
                        ctx.need(sc, (tag, "B_pvB", qc, a))
                        if a >= 1:
                            ctx.need(sc, (tag, "B_aoB", qc, a - 1))
                        elif qc >= 1:
                            ctx.need(sc, (tag, "B_aoB", qc - 1, 7))
                        ctx.inc(sc.copy(oB[:], ps[5][0:64, :]), "ACT")
                        ctx.mark((tag, "B_oevB", qc, a), "ACT")

            def sB_tensor(t):
                ctx.need(t, (tag, "B_const"))
                ctx.need(t, (tag, "A_krow"))
                for qc in range(nqc):
                    ctx.need(t, (tag, "B_cp", qc))
                    ctx.need(t, (tag, "B_sq", qc))
                    if qc >= 1:
                        ctx.need(t, (tag, "B_var", qc - 1))
                        ctx.need(t, (tag, "B_x1", qc - 1))
                    for dc in range(8):
                        ctx.inc(t.matmul(ps[0][0:1, :], ctx.ones_b,
                                         xbf[:, dc * 512:(dc + 1) * 512],
                                         start=(dc == 0), stop=(dc == 7)), "PE")
                        ctx.inc(t.matmul(ps[1][0:1, :], ctx.ones_b,
                                         xsq[:, dc * 512:(dc + 1) * 512],
                                         start=(dc == 0), stop=(dc == 7)), "PE")
                    ctx.mark((tag, "B_sm", qc), "PE")
                    ctx.need(t, (tag, "B_var", qc))
                    if qc >= 1:
                        ctx.need(t, (tag, "B_x1", qc - 1))
                    ctx.inc(t.matmul(ps[6][:], ctx.ones_f, r_mu,
                                     start=True, stop=True), "PE")
                    ctx.need(t, (tag, "B_rc", qc))
                    ctx.inc(t.matmul(ps[7][:], ctx.ones_f, r_c,
                                     start=True, stop=True), "PE")
                    ctx.mark((tag, "B_bc", qc), "PE")
                    ctx.need(t, (tag, "B_h", qc))
                    for a in range(8):
                        if a >= 2:
                            ctx.need(t, (tag, "B_qevac", qc, a - 2))
                        for dc in range(8):
                            ctx.inc(t.matmul(ps[a % 2][:],
                                             wq[:, dc * 1024 + a * 128:
                                                dc * 1024 + (a + 1) * 128],
                                             hq[:, dc * 512:(dc + 1) * 512],
                                             start=(dc == 0), stop=(dc == 7)), "PE")
                        ctx.mark((tag, "B_qp", qc, a), "PE")
                        ctx.need(t, (tag, "B_qevac", qc, a))
                        if a >= 2:
                            ctx.need(t, (tag, "B_qrope", qc, a - 2))
                        ctx.inc(t.matmul(ps[2 + a % 2][:], rotP[:],
                                         [qraw0, qraw1][a % 2][:],
                                         start=True, stop=True), "PE")
                        ctx.mark((tag, "B_rot", qc, a), "PE")
                    for a in range(8):
                        ctx.need(t, (tag, "B_qsq", qc, a))
                        if a == 0:
                            ctx.need(t, (tag, "B_h", qc))
                        else:
                            ctx.need(t, (tag, "B_qn2r", qc, a - 1))
                        ctx.inc(t.matmul(ps[7][0:2, :], pair[:],
                                         [qsq0, qsq1][a % 2][:],
                                         start=True, stop=True), "PE")
                        ctx.mark((tag, "B_pair", qc, a), "PE")
                    ctx.need(t, (tag, "B_mneg", qc))
                    ctx.inc(t.matmul(ps[6][:, 0:16], ctx.ones_f, mneg[:],
                                     start=True, stop=True), "PE")
                    ctx.mark((tag, "B_mb", qc), "PE")
                    for a in range(8):
                        c = a // 2
                        nkt = KT(qc)
                        kvA, kvB = c, c + 4

                        def pv_pair(kt, a=a, c=c, nkt=nkt, kvA=kvA, kvB=kvB):
                            if diag(qc, kt):
                                ctx.need(t, (tag, "B_mA", qc, a, kt))
                            else:
                                ctx.need(t, (tag, "B_expA", qc, a, kt))
                            if kt == 0 and (a >= 1 or qc >= 1):
                                if a >= 1:
                                    ctx.need(t, (tag, "B_oevA", qc, a - 1))
                                else:
                                    ctx.need(t, (tag, "B_oevA", qc - 1, 7))
                            ctx.inc(t.matmul(
                                ps[4][0:65, :], vsb[:, kt * 520 + kvA * 65:
                                                    kt * 520 + kvA * 65 + 65],
                                PA[kt % 3][:],
                                start=(kt == 0), stop=(kt == nkt - 1)), "PE")
                            ctx.mark((tag, "B_pvAk", qc, a, kt), "PE")
                            if diag(qc, kt):
                                ctx.need(t, (tag, "B_mB", qc, a, kt))
                            else:
                                ctx.need(t, (tag, "B_expB", qc, a, kt))
                            if kt == 0 and (a >= 1 or qc >= 1):
                                if a >= 1:
                                    ctx.need(t, (tag, "B_oevB", qc, a - 1))
                                else:
                                    ctx.need(t, (tag, "B_oevB", qc - 1, 7))
                            ctx.inc(t.matmul(
                                ps[5][0:65, :], vsb[:, kt * 520 + kvB * 65:
                                                    kt * 520 + kvB * 65 + 65],
                                PB[kt % 3][:],
                                start=(kt == 0), stop=(kt == nkt - 1)), "PE")
                            ctx.mark((tag, "B_pvBk", qc, a, kt), "PE")

                        for kt in range(nkt):
                            if kt >= 2:
                                ctx.need(t, (tag, "B_expA", qc, a, kt - 2))
                                ctx.need(t, (tag, "B_expB", qc, a, kt - 2))
                            elif a >= 1:
                                ctx.need(t, (tag, "B_expA", qc, a - 1, KT(qc) - 2 + kt))
                                ctx.need(t, (tag, "B_expB", qc, a - 1, KT(qc) - 2 + kt))
                            else:
                                ctx.need(t, (tag, "B_qrope", qc, 7))
                                if qc >= 1:
                                    ctx.need(t, (tag, "B_expA", qc - 1, 7, KT(qc - 1) - 2 + kt))
                                    ctx.need(t, (tag, "B_expB", qc - 1, 7, KT(qc - 1) - 2 + kt))
                            ctx.inc(t.matmul(
                                ps[2 + kt % 2][:], ksb[0:64, c * T + kt * 128:
                                                       c * T + (kt + 1) * 128],
                                qhat[0:64, a * 512:(a + 1) * 512],
                                start=True, stop=True), "PE")
                            ctx.mark((tag, "B_sA", qc, a, kt), "PE")
                            ctx.inc(t.matmul(
                                ps[0 + kt % 2][:], ksb[64:128, c * T + kt * 128:
                                                       c * T + (kt + 1) * 128],
                                qhat[64:128, a * 512:(a + 1) * 512],
                                start=True, stop=True), "PE")
                            ctx.mark((tag, "B_sB", qc, a, kt), "PE")
                            if kt >= 2:
                                pv_pair(kt - 2)
                        pv_pair(nkt - 2)
                        pv_pair(nkt - 1)
                        ctx.mark((tag, "B_pvA", qc, a), "PE")
                        ctx.mark((tag, "B_pvB", qc, a), "PE")
                        ctx.need(t, (tag, "B_rbnA", qc, a))
                        if a == 0:
                            ctx.need(t, (tag, "B_mbs", qc))
                        else:
                            ctx.need(t, (tag, "B_aoB", qc, a - 1))
                        ctx.inc(t.matmul(ps[6][:], ctx.ones_f, rbn[:, 0:512],
                                         start=True, stop=True), "PE")
                        ctx.mark((tag, "B_bnA", qc, a), "PE")
                        ctx.need(t, (tag, "B_rbnB", qc, a))
                        ctx.need(t, (tag, "B_aoA", qc, a))
                        ctx.inc(t.matmul(ps[6][:], ctx.ones_f, rbn[:, 512:1024],
                                         start=True, stop=True), "PE")
                        ctx.mark((tag, "B_bnB", qc, a), "PE")
                    ctx.need(t, (tag, "B_aoB", qc, 7))
                    ctx.inc(t.matmul(ps[6][:], ctx.ones_f, r_mu,
                                     start=True, stop=True), "PE")
                    ctx.need(t, (tag, "B_qn2r", qc, 7))
                    ctx.inc(t.matmul(ps[7][:], ctx.ones_f, r_r,
                                     start=True, stop=True), "PE")
                    ctx.mark((tag, "B_ybc", qc), "PE")
                    for oc2 in range(8):
                        if oc2 < 2:
                            ctx.need(t, (tag, "B_expB", qc, 7, KT(qc) - 1))
                        if oc2 >= 2:
                            ctx.need(t, (tag, "B_x1p", qc, oc2 - 2))
                        for a in range(8):
                            ctx.inc(t.matmul(
                                ps[oc2 % 2][:],
                                wo[:, a * 1024 + oc2 * 128:a * 1024 + (oc2 + 1) * 128],
                                ao[:, a * 512:(a + 1) * 512],
                                start=(a == 0), stop=(a == 7)), "PE")
                        ctx.mark((tag, "B_wo", qc, oc2), "PE")

            if "B" in parts:
                run_two_pass(ctx, [("sync", sB_sync), ("gpsimd", sB_gpsimd),
                                   ("vector", sB_vector), ("scalar", sB_scalar),
                                   ("tensor", sB_tensor)])
        barrier(ctx, tag + "B")
        stKV.close()

        with contextlib.ExitStack() as stC:
            def sbC(name, shape, dt):
                return stC.enter_context(nc.sbuf_tensor(tag + "C" + name, shape, dt))
            X1q = sbC("_x1", [128, 8 * QW], F32)
            Xacc = sbC("_xa", [128, 8 * QW], F32)
            h2 = sbC("_h2", [128, 8 * QW], BF16)
            ze = sbC("_ze", [128, 8 * QW], BF16)
            act = sbC("_act", [128, 28 * QW], BF16)
            x1sq = sbC("_sq", [128, 8 * 512], BF16)
            wgt = [sbC("_wg%d" % i, [128, 8 * 128], BF16) for i in range(2)]
            wut = [sbC("_wu%d" % i, [128, 8 * 128], BF16) for i in range(2)]
            wdt = [sbC("_wd%d" % i, [128, 28 * 128], BF16) for i in range(2)]
            sil = [sbC("_sil%d" % i, [128, 512], BF16) for i in range(2)]
            rw = sbC("_rw", [128, 64], BF16)
            ln2c = sbC("_ln2", [128, 8], F32)
            crows = sbC("_rows", [1, 4 * 512], F32)
            negm = sbC("_negm", [128, 1], F32)
            m1c = sbC("_m1", [128, 4], F32)
            probs = sbC("_pr", [128, 3 * 8], F32)
            wtok = sbC("_wt", [128, (QW // 128) * 8], F32)
            wfm = sbC("_wfm", [8, QW], F32)
            wer = [sbC("_wer%d" % i, [1, QW], F32) for i in range(2)]
            r_s2 = crows[:, 0:512]
            r_csq = crows[:, 512:1024]
            r_r2 = [crows[:, 1024:1536], crows[:, 1536:2048]]
            NTL = QW // 128

            def sC_sync(s):
                for qq in range(nquart):
                    i = ap["X1"][:, :, qq * QW:(qq + 1) * QW] \
                        .rearrange("c p n -> p c n")
                    ctx.inc(s.dma_start(
                        X1q[:].rearrange("p (c n) -> p c n", n=QW), i),
                        ["h4", "h5"][qq % 2], 16)
                    ctx.mark((tag, "C_x1l", qq), ["h4", "h5"][qq % 2])
                    i = ap["X"][:, :, qq * QW:(qq + 1) * QW] \
                        .rearrange("c p n -> p c n")
                    ctx.inc(s.dma_start(
                        Xacc[:].rearrange("p (c n) -> p c n", n=QW), i),
                        ["h6", "h7"][qq % 2], 16)
                    ctx.mark((tag, "C_xl", qq), ["h6", "h7"][qq % 2])
                    ctx.need(s, (tag, "C_done", qq))
                    o = ap["X"][:, :, qq * QW:(qq + 1) * QW] \
                        .rearrange("c p n -> p c n")
                    ctx.inc(s.dma_start(o, Xacc[:].rearrange("p (c n) -> p c n", n=QW)),
                            ["h2", "h3"][qq % 2], 16)
                    s.wait_ge(ctx.sem[["h2", "h3"][qq % 2]],
                              ctx.c[["h2", "h3"][qq % 2]])

            def sC_gpsimd(g):
                if ("p0", "moe") in ctx.sched:
                    ctx.need(g, ("p0", "moe"))
                ctx.inc(g.dma_start(rw[:].rearrange("p (z e) -> p z e", e=8),
                                    ap["L0_ffn_router_w"]
                                    .rearrange("(z p) e -> p z e", p=128)), "g3", 16)
                ctx.inc(g.dma_start(ln2c[:], ap["L0_ffn_ln2_w"]
                                    .rearrange("(c p) -> p c", p=128),
                                    allow_slow_non_contiguous=True), "g3", 16)
                ctx.mark((tag, "C_const"), "g3")
                for qq in range(nquart):
                    ctx.need(g, (tag, "C_wfm", qq))
                    ctx.inc(g.dma_start(ap["s_small"][0:8, 0:QW], wfm[:]), "g2", 16)
                    g.wait_ge(ctx.sem["g2"], ctx.c["g2"])
                    ctx.mark((tag, "C_wbo", qq), "g2")
                    for e in range(8):
                        if e >= 2:
                            ctx.need(g, (tag, "C_ze", qq, e - 2))
                        ctx.inc(g.dma_start(wer[e % 2][:], ap["s_small"][e:e + 1, 0:QW]),
                                ["g2", "g10"][e % 2], 16)
                        ctx.mark((tag, "C_wer", qq, e), ["g2", "g10"][e % 2])
                        for mc in range(28):
                            if e > 0 or mc >= 2:
                                pm = (tag, "C_gu", qq, e, mc - 2, NHF - 1) if mc >= 2 \
                                    else (tag, "C_gu", qq, e - 1, mc + 26, NHF - 1)
                                ctx.need(g, pm)
                            src = ap["s_g"][e][:, mc * 128:(mc + 1) * 128] \
                                .rearrange("(z p) m -> p z m", p=128)
                            ctx.inc(g.dma_start(
                                wgt[mc % 2][:].rearrange("p (z m) -> p z m", m=128),
                                src), ["g0", "g1"][mc % 2], 16)
                            ctx.mark((tag, "C_wg", qq, e, mc), ["g0", "g1"][mc % 2])
                            src = ap["s_u"][e][:, mc * 128:(mc + 1) * 128] \
                                .rearrange("(z p) m -> p z m", p=128)
                            ctx.inc(g.dma_start(
                                wut[mc % 2][:].rearrange("p (z m) -> p z m", m=128),
                                src), ["g8", "g9"][mc % 2], 16)
                            ctx.mark((tag, "C_wu", qq, e, mc), ["g8", "g9"][mc % 2])
                        for dc in range(8):
                            if e > 0 or dc >= 2:
                                pm = (tag, "C_dn", qq, e, dc - 2, NHF - 1) if dc >= 2 \
                                    else (tag, "C_dn", qq, e - 1, dc + 6, NHF - 1)
                                ctx.need(g, pm)
                            src = ap["s_d"][e][:, dc * 128:(dc + 1) * 128] \
                                .rearrange("(z p) m -> p z m", p=128)
                            ctx.inc(g.dma_start(
                                wdt[dc % 2][:].rearrange("p (z m) -> p z m", m=128),
                                src), ["g6", "g11"][dc % 2], 16)
                            ctx.mark((tag, "C_wd", qq, e, dc), ["g6", "g11"][dc % 2])

            def sC_vector(v):
                for qq in range(nquart):
                    ctx.need(v, (tag, "C_x1l", qq))
                    ctx.need(v, (tag, "C_xl", qq))
                    for dc in range(8):
                        ctx.inc(v.scalar_tensor_tensor(
                            Xacc[:, dc * QW:(dc + 1) * QW],
                            X1q[:, dc * QW:(dc + 1) * QW], 0.5,
                            Xacc[:, dc * QW:(dc + 1) * QW], ALU.mult, ALU.add), "DVE")
                    ctx.mark((tag, "C_xacc", qq), "DVE")
                    for hf in range(NHF):
                        ctx.need(v, (tag, "C_sm", qq, hf))
                        ctx.fence(v, "DVE")
                        ctx.inc(v.tensor_scalar_mul(r_s2, ps[7][0:1, :], 1.0 / 1024),
                                "DVE")
                        ctx.mark((tag, "C_s2m", qq, hf), "DVE")
                        ctx.need(v, (tag, "C_sqrt", qq, hf))
                        ctx.inc(v.reciprocal(r_r2[hf], r_csq), "DVE")
                        ctx.mark((tag, "C_r2", qq, hf), "DVE")
                        ctx.need(v, (tag, "C_r2bc", qq, hf))
                        for dc in range(8):
                            ctx.inc(v.scalar_tensor_tensor(
                                h2[:, dc * QW + hf * 512:dc * QW + hf * 512 + 512],
                                X1q[:, dc * QW + hf * 512:dc * QW + hf * 512 + 512],
                                ln2c[:, dc:dc + 1], ps[6][:], ALU.mult, ALU.mult),
                                "DVE")
                        ctx.mark((tag, "C_h2", qq, hf), "DVE")
                    for tt in range(NTL):
                        ctx.need(v, (tag, "C_rt", qq, tt))
                        ctx.inc(v.reduce_max(negm[:], ps[7][:, 0:8], axis=AX,
                                             negate=True), "DVE")
                        ctx.mark((tag, "C_negm", qq, tt), "DVE")
                        ctx.need(v, (tag, "C_rexp", qq, tt))
                        pr0 = probs[:, 0:8]; pr1 = probs[:, 8:16]; pr2 = probs[:, 16:24]
                        ctx.inc(v.reduce_sum(m1c[:, 0:1], pr0, axis=AX), "DVE")
                        ctx.fence(v, "DVE")
                        ctx.inc(v.reciprocal(m1c[:, 1:2], m1c[:, 0:1]), "DVE")
                        ctx.fence(v, "DVE")
                        ctx.inc(v.tensor_scalar_mul(pr1, pr0, m1c[:, 1:2]), "DVE")
                        ctx.fence(v, "DVE")
                        ctx.inc(v.reduce_max(m1c[:, 0:1], pr1, axis=AX), "DVE")
                        ctx.fence(v, "DVE")
                        ctx.inc(v.tensor_single_scalar(pr2, pr1, m1c[:, 0:1],
                                                       ALU.is_ge), "DVE")
                        ctx.fence(v, "DVE")
                        ctx.inc(v.scalar_tensor_tensor(pr2, pr2, -1e30, pr1,
                                                       ALU.mult, ALU.add), "DVE")
                        ctx.fence(v, "DVE")
                        ctx.inc(v.reduce_max(m1c[:, 2:3], pr2, axis=AX), "DVE")
                        ctx.fence(v, "DVE")
                        ctx.inc(v.tensor_single_scalar(pr2, pr1, m1c[:, 2:3],
                                                       ALU.is_ge), "DVE")
                        ctx.fence(v, "DVE")
                        ctx.inc(v.tensor_tensor(pr1, pr1, pr2, ALU.mult), "DVE")
                        ctx.inc(v.tensor_tensor(m1c[:, 3:4], m1c[:, 0:1], m1c[:, 2:3],
                                                ALU.add), "DVE")
                        ctx.fence(v, "DVE")
                        ctx.inc(v.reciprocal(m1c[:, 3:4], m1c[:, 3:4]), "DVE")
                        ctx.fence(v, "DVE")
                        ctx.inc(v.tensor_scalar_mul(wtok[:, tt * 8:(tt + 1) * 8],
                                                    pr1, m1c[:, 3:4]), "DVE")
                        ctx.mark((tag, "C_wf", qq, tt), "DVE")
                    for e in range(8):
                        for hf in range(NHF):
                            ctx.need(v, (tag, "C_webc", qq, e, hf))
                            for dc in range(8):
                                ctx.inc(v.tensor_tensor(
                                    ze[:, dc * QW + hf * 512:dc * QW + hf * 512 + 512],
                                    h2[:, dc * QW + hf * 512:dc * QW + hf * 512 + 512],
                                    ps[6][:], ALU.mult), "DVE")
                            ctx.mark((tag, "C_zeh", qq, e, hf), "DVE")
                        ctx.mark((tag, "C_ze", qq, e), "DVE")
                        for mc in range(28):
                            for hf in range(NHF):
                                u = mc * NHF + hf
                                ctx.need(v, (tag, "C_sil", qq, e, mc, hf))
                                ctx.need(v, (tag, "C_gu", qq, e, mc, hf))
                                ctx.inc(v.tensor_tensor(
                                    act[:, mc * QW + hf * 512:mc * QW + hf * 512 + 512],
                                    sil[u % 2][:], ps[2 + u % 2][:], ALU.mult), "DVE")
                                ctx.mark((tag, "C_act", qq, e, mc, hf), "DVE")
                        for dc in range(8):
                            for hf in range(NHF):
                                ctx.need(v, (tag, "C_dn", qq, e, dc, hf))
                                if e >= 1:
                                    ctx.need(v, (tag, "C_ev", qq, e - 1, dc, hf))
                                ctx.inc(v.scalar_tensor_tensor(
                                    Xacc[:, dc * QW + hf * 512:dc * QW + hf * 512 + 512],
                                    ps[4 + (dc * NHF + hf) % 2][:], 0.5,
                                    Xacc[:, dc * QW + hf * 512:dc * QW + hf * 512 + 512],
                                    ALU.mult, ALU.add), "DVE")
                                ctx.mark((tag, "C_ev", qq, e, dc, hf), "DVE")
                    ctx.mark((tag, "C_done", qq), "DVE")

            def sC_scalar(sc):
                for qq in range(nquart):
                    ctx.need(sc, (tag, "C_x1l", qq))
                    for hf in range(NHF):
                        if hf >= 1 or qq >= 1:
                            pm = (tag, "C_sm", qq, hf - 1) if hf >= 1 \
                                else (tag, "C_sm", qq - 1, NHF - 1)
                            ctx.need(sc, pm)
                        for dc in range(8):
                            ctx.inc(sc.activation(
                                x1sq[:, dc * 512:(dc + 1) * 512],
                                X1q[:, dc * QW + hf * 512:dc * QW + hf * 512 + 512],
                                AF.Square), "ACT")
                        ctx.mark((tag, "C_sqs", qq, hf), "ACT")
                        ctx.need(sc, (tag, "C_s2m", qq, hf))
                        ctx.inc(sc.activation(r_csq, r_s2, AF.Sqrt,
                                              bias=ctx.constf(1e-6)[0:1, :]), "ACT")
                        ctx.mark((tag, "C_sqrt", qq, hf), "ACT")
                    for tt in range(NTL):
                        ctx.need(sc, (tag, "C_negm", qq, tt))
                        if tt >= 1:
                            ctx.need(sc, (tag, "C_wf", qq, tt - 1))
                        elif qq >= 1:
                            ctx.need(sc, (tag, "C_wf", qq - 1, NTL - 1))
                        ctx.inc(sc.activation(probs[:, 0:8], ps[7][:, 0:8], AF.Exp,
                                              bias=negm[:]), "ACT")
                        ctx.mark((tag, "C_rexp", qq, tt), "ACT")
                        ctx.need(sc, (tag, "C_tr", qq, tt))
                        ctx.inc(sc.copy(wfm[:, tt * 128:(tt + 1) * 128],
                                        ps[6][0:8, 0:128]), "ACT")
                        ctx.mark((tag, "C_wfme", qq, tt), "ACT")
                        if ctx.recording and tt == NTL - 1:
                            ctx.sched[(tag, "C_wfm", qq)] = \
                                ctx.sched[(tag, "C_wfme", qq, tt)]
                    for e in range(8):
                        for mc in range(28):
                            for hf in range(NHF):
                                u = mc * NHF + hf
                                ctx.need(sc, (tag, "C_g", qq, e, mc, hf))
                                if u >= 2:
                                    ctx.need(sc, (tag, "C_act", qq, e,
                                                  (u - 2) // NHF, (u - 2) % NHF))
                                elif e >= 1:
                                    ctx.need(sc, (tag, "C_act", qq, e - 1, 27, NHF - 1))
                                elif qq >= 1:
                                    ctx.need(sc, (tag, "C_act", qq - 1, 7, 27, NHF - 1))
                                ctx.inc(sc.activation(sil[u % 2][:],
                                                      ps[u % 2][:], AF.Silu), "ACT")
                                ctx.mark((tag, "C_sil", qq, e, mc, hf), "ACT")

            def sC_tensor(t):
                ctx.need(t, (tag, "C_const"))
                for qq in range(nquart):
                    for hf in range(NHF):
                        ctx.need(t, (tag, "C_sqs", qq, hf))
                        if hf >= 1 or qq >= 1:
                            pm = (tag, "C_s2m", qq, hf - 1) if hf >= 1 \
                                else (tag, "C_s2m", qq - 1, NHF - 1)
                            ctx.need(t, pm)
                        for dc in range(8):
                            ctx.inc(t.matmul(ps[7][0:1, :], ctx.ones_b,
                                             x1sq[:, dc * 512:(dc + 1) * 512],
                                             start=(dc == 0), stop=(dc == 7)), "PE")
                        ctx.mark((tag, "C_sm", qq, hf), "PE")
                        ctx.need(t, (tag, "C_r2", qq, hf))
                        if hf >= 1:
                            ctx.need(t, (tag, "C_h2", qq, hf - 1))
                        elif qq >= 1:
                            ctx.need(t, (tag, "C_ev", qq - 1, 7, 7, NHF - 1))
                        ctx.inc(t.matmul(ps[6][:], ctx.ones_f, r_r2[hf],
                                         start=True, stop=True), "PE")
                        ctx.mark((tag, "C_r2bc", qq, hf), "PE")
                    for tt in range(NTL):
                        ctx.need(t, (tag, "C_h2", qq, NHF - 1))
                        if tt >= 1:
                            ctx.need(t, (tag, "C_rexp", qq, tt - 1))
                        else:
                            ctx.need(t, (tag, "C_s2m", qq, NHF - 1))
                        for dc in range(8):
                            ctx.inc(t.matmul(
                                ps[7][:, 0:8],
                                h2[:, dc * QW + tt * 128:dc * QW + (tt + 1) * 128],
                                rw[:, dc * 8:(dc + 1) * 8],
                                start=(dc == 0), stop=(dc == 7)), "PE")
                        ctx.mark((tag, "C_rt", qq, tt), "PE")
                        ctx.need(t, (tag, "C_wf", qq, tt))
                        if tt >= 1:
                            ctx.need(t, (tag, "C_wfme", qq, tt - 1))
                        ctx.inc(t.transpose(ps[6][0:8, 0:128],
                                            wtok[:, tt * 8:(tt + 1) * 8],
                                            ctx.cident_f), "PE")
                        ctx.mark((tag, "C_tr", qq, tt), "PE")
                    for e in range(8):
                        for hf in range(NHF):
                            ctx.need(t, (tag, "C_wer", qq, e))
                            if e == 0 and hf == 0:
                                ctx.need(t, (tag, "C_wfme", qq, NTL - 1))
                            elif hf == 0:
                                ctx.need(t, (tag, "C_ze", qq, e - 1))
                            else:
                                ctx.need(t, (tag, "C_zeh", qq, e, hf - 1))
                            ctx.inc(t.matmul(ps[6][:], ctx.ones_f,
                                             wer[e % 2][:, hf * 512:hf * 512 + 512],
                                             start=True, stop=True), "PE")
                            ctx.mark((tag, "C_webc", qq, e, hf), "PE")
                        for mc in range(28):
                            for hf in range(NHF):
                                u = mc * NHF + hf
                                ctx.need(t, (tag, "C_wg", qq, e, mc))
                                if u >= 2:
                                    ctx.need(t, (tag, "C_sil", qq, e,
                                                 (u - 2) // NHF, (u - 2) % NHF))
                                elif e >= 1:
                                    ctx.need(t, (tag, "C_sil", qq, e - 1, 27, NHF - 1))
                                elif qq >= 1:
                                    ctx.need(t, (tag, "C_sil", qq - 1, 7, 27, NHF - 1))
                                for dc in range(8):
                                    ctx.inc(t.matmul(
                                        ps[u % 2][:],
                                        wgt[mc % 2][:, dc * 128:(dc + 1) * 128],
                                        h2[:, dc * QW + hf * 512:dc * QW + hf * 512 + 512],
                                        start=(dc == 0), stop=(dc == 7)), "PE")
                                ctx.mark((tag, "C_g", qq, e, mc, hf), "PE")
                                ctx.need(t, (tag, "C_wu", qq, e, mc))
                                ctx.need(t, (tag, "C_ze", qq, e))
                                if u >= 2:
                                    ctx.need(t, (tag, "C_act", qq, e,
                                                 (u - 2) // NHF, (u - 2) % NHF))
                                elif e >= 1:
                                    ctx.need(t, (tag, "C_act", qq, e - 1, 27, NHF - 1))
                                elif qq >= 1:
                                    ctx.need(t, (tag, "C_act", qq - 1, 7, 27, NHF - 1))
                                for dc in range(8):
                                    ctx.inc(t.matmul(
                                        ps[2 + u % 2][:],
                                        wut[mc % 2][:, dc * 128:(dc + 1) * 128],
                                        ze[:, dc * QW + hf * 512:dc * QW + hf * 512 + 512],
                                        start=(dc == 0), stop=(dc == 7)), "PE")
                                ctx.mark((tag, "C_gu", qq, e, mc, hf), "PE")
                        for dc in range(8):
                            for hf in range(NHF):
                                w = dc * NHF + hf
                                ctx.need(t, (tag, "C_wd", qq, e, dc))
                                ctx.need(t, (tag, "C_act", qq, e, 27, hf))
                                if w >= 2:
                                    ctx.need(t, (tag, "C_ev", qq, e,
                                                 (w - 2) // NHF, (w - 2) % NHF))
                                elif e >= 1:
                                    ctx.need(t, (tag, "C_ev", qq, e - 1, 7, NHF - 1))
                                elif qq >= 1:
                                    ctx.need(t, (tag, "C_ev", qq - 1, 7, 7, NHF - 1))
                                for mc in range(28):
                                    ctx.inc(t.matmul(
                                        ps[4 + w % 2][:],
                                        wdt[dc % 2][:, mc * 128:(mc + 1) * 128],
                                        act[:, mc * QW + hf * 512:mc * QW + hf * 512 + 512],
                                        start=(mc == 0), stop=(mc == 27)), "PE")
                                ctx.mark((tag, "C_dn", qq, e, dc, hf), "PE")

            if "C" in parts:
                run_two_pass(ctx, [("sync", sC_sync), ("gpsimd", sC_gpsimd),
                                   ("vector", sC_vector), ("scalar", sC_scalar),
                                   ("tensor", sC_tensor)])
        barrier(ctx, tag + "C")



def p_mamba(ctx):
    nc, T, ap = ctx.nc, ctx.T, ctx.ap
    ntc = T // 512
    NOC = 35

    with contextlib.ExitStack() as st:
        def sb(name, shape, dt):
            return st.enter_context(nc.sbuf_tensor("p5" + name, shape, dt))

        def pt(name):
            return st.enter_context(nc.psum_tensor("p5" + name, [128, 512], F32))
        Xt = sb("_Xt", [128, 8 * 512], F32)
        ztc = sb("_z", [128, 16 * 512], BF16)
        xbf = ztc[:, 0:4096]
        xsq = ztc[:, 4096:8192]
        gb = sb("_gb", [128, 8 * 512], BF16)
        xbc = sb("_xbc", [128, 18 * 515], BF16)
        xs = sb("_xs", [128, 18 * 512], BF16)
        acc0 = sb("_ac0", [128, 512], F32)
        acc1 = sb("_ac1", [128, 512], F32)
        dtraw = sb("_dtr", [32, 512], F32)
        dtv = sb("_dtv", [32, 512], F32)
        dw1 = sb("_dw1", [32, 512], F32)
        dw2 = sb("_dw2", [32, 512], F32)
        lnrow = sb("_lnr", [32, 512], F32)
        aex = sb("_aex", [32, 1], F32)
        Wc = sb("_W", [32, 512], F32)
        wip = [sb("_wi%d" % i, [128, 8 * 128], BF16) for i in range(2)]
        woph = [sb("_wo%d" % i, [128, 8 * 128], BF16) for i in range(3)]
        WOSEM = ["g5", "g6", "g4"]
        cw = sb("_cw", [128, 18 * 4], F32)
        cb = sb("_cb", [128, 18], F32)
        dtb = sb("_dtb", [32, 1], F32)
        alog = sb("_alog", [32, 1], F32)
        acol = sb("_acol", [32, 1], F32)
        dcol = sb("_dcol", [128, 16], F32)
        d32 = sb("_d32", [32, 1], F32)
        drow = sb("_drow", [1, 32], F32)
        nw = sb("_nw", [128, 16], F32)
        ST_ = sb("_ST", [128, 16 * 128], F32)
        STb = sb("_STb", [128, 16 * 128], BF16)
        ytc = sb("_y", [128, 16 * 512], BF16)
        Wrow = sb("_wr", [1, 4096], F32)
        endr = sb("_endr", [1, 32], F32)
        endp = sb("_endp", [1, 32], F32)
        lrow = sb("_lrow", [1, 32], F32)
        lamb_sb = sb("_lmb", [128, 32], F32)
        endc = sb("_endc", [128, 32], F32)
        wtl = sb("_wtl", [128, 32], F32)
        WTc = sb("_WT", [128, 32], F32)
        dtT = sb("_dtT", [128, 32], F32)
        Btm = sb("_Btm", [128, 128], BF16)
        Gsb = sb("_G", [128, 128], F32)
        Dsb = [sb("_D%d" % i, [128, 512], F32) for i in range(2)]
        Dsb2 = [sb("_D2%d" % i, [128, 512], F32) for i in range(2)]
        Ealy = [sb("_E%d" % i, [128, 512], BF16) for i in range(2)]
        expw = [sb("_xw%d" % i, [128, 512], BF16) for i in range(2)]
        Ptl = sb("_Pt", [128, 4096], BF16)
        Chat = sb("_Ch", [128, 4096], BF16)
        Bhat = sb("_Bh", [128, 4096], BF16)
        xsT = sb("_xsT", [128, 2048], BF16)
        xdtT = sb("_xdT", [128, 2048], BF16)
        trib = sb("_tri", [128, 128], BF16)
        zzero = sb("_zz", [32, 512], F32)
        zsil = sb("_zs", [128, 512], BF16)
        zsil2 = sb("_zs2", [128, 512], BF16)
        rows = sb("_rows", [1, 4 * 512], F32)
        ps = [pt("_ps%d" % i) for i in range(8)]
        r_mu = rows[:, 0:512]; r_m2 = rows[:, 512:1024]
        r_var = rows[:, 1024:1536]; r_r = rows[:, 1536:2048]
        r_sq = rows[:, 1024:1536]; r_s2 = rows[:, 512:1024]

        def s5_sync(s):
            for tc in range(ntc):
                if tc >= 1:
                    ctx.need(s, (5, "store", tc - 1))
                i = ap["X"][:, :, tc * 512:(tc + 1) * 512].rearrange("c p n -> p c n")
                ctx.inc(s.dma_start(Xt[:].rearrange("p (c n) -> p c n", n=512), i),
                        ["h0", "h1"][tc % 2], 16)
                ctx.mark((5, "xl", tc), ["h0", "h1"][tc % 2])
                ctx.need(s, (5, "Wdone", tc))
                ctx.inc(s.dma_start(ap["s_small"][8:40, 0:512], Wc[:]), "h2", 16)
                s.wait_ge(ctx.sem["h2"], ctx.c["h2"])
                for c in range(4):
                    if tc > 0 or c > 0:
                        pc = (5, "wrowfree", tc, c - 1) if c > 0 \
                            else (5, "wrowfree", tc - 1, 3)
                        ctx.need(s, pc)
                    srcap = bass.AP(ap["s_small"].tensor, 8 * 4096 + c * 128,
                                    [[4096, 1], [4096, 32], [1, 128]])
                    ctx.inc(s.dma_start(
                        Wrow[:].rearrange("o (h t) -> o h t", t=128), srcap),
                        "h3", 16)
                    s.wait_ge(ctx.sem["h3"], ctx.c["h3"])
                    ctx.mark((5, "wrow", tc, c), "h3")

        def s5_gpsimd(g):
            ctx.inc(g.dma_start(cw[:].rearrange("p (z k) -> p z k", k=4),
                                ap["L1_attn_conv_w"].rearrange("(z p) k -> p z k",
                                                               p=128)), "g3", 16)
            ctx.inc(g.dma_start(cb[:], ap["L1_attn_conv_b"]
                                .rearrange("(c p) -> p c", p=128),
                                allow_slow_non_contiguous=True), "g3", 16)
            ctx.inc(g.dma_start(dtb[:], ap["L1_attn_dt_bias"]
                                .rearrange("(p o) -> p o", o=1)), "g3", 16)
            ctx.inc(g.dma_start(alog[:], ap["L1_attn_A_log"]
                                .rearrange("(p o) -> p o", o=1)), "g3", 16)
            ctx.inc(g.dma_start(d32[:], ap["L1_attn_D"]
                                .rearrange("(p o) -> p o", o=1)), "g3", 16)
            ctx.inc(g.dma_start(nw[:], ap["L1_attn_norm_w"]
                                .rearrange("(c p) -> p c", p=128),
                                allow_slow_non_contiguous=True), "g3", 16)
            ctx.inc(g.dma_start(trib[:], ap["c_tri"]), "g3", 16)
            ctx.inc(g.dma_start(ctx.ch0, ap["c_h0"]), "g3", 16)
            ctx.inc(g.dma_start(ctx.ch1, ap["c_h1"]), "g3", 16)
            g.memset(xbc[:], 0.0)
            g.memset(zzero[:], 0.0)
            g.memset(ST_[:], 0.0)
            ins = g.memset(STb[:], 0.0)
            ctx.inc(ins, "POOL")
            ctx.mark((5, "const"), "g3")
            ctx.mark((5, "init"), "POOL")
            for tc in range(ntc):
                for oc in range(NOC):
                    if tc > 0 or oc >= 2:
                        pm = (5, "ipm", tc, oc - 2) if oc >= 2 \
                            else (5, "ipm", tc - 1, oc + NOC - 2)
                        ctx.need(g, pm)
                    w = 128 if oc < 34 else 32
                    src = ap["s_ipT"][:, oc * 128:oc * 128 + w] \
                        .rearrange("(z p) m -> p z m", p=128)
                    dst = bass.AP(wip[oc % 2], 0, [[8 * 128, 128], [128, 8], [1, w]])
                    ctx.inc(g.dma_start(dst, src), ["g0", "g1"][oc % 2], 16)
                    ctx.mark((5, "wipl", tc, oc), ["g0", "g1"][oc % 2])
                for u in range(16):
                    oc, half = u // 2, u % 2
                    if tc > 0 or u >= 3:
                        pm = (5, "opu", tc, u - 3) if u >= 3 \
                            else (5, "opu", tc - 1, u + 13)
                        ctx.need(g, pm)
                    src = ap["s_opT"][half * 1024:half * 1024 + 1024,
                                      oc * 128:(oc + 1) * 128] \
                        .rearrange("(z p) m -> p z m", p=128)
                    ctx.inc(g.dma_start(
                        woph[u % 3][:].rearrange("p (z m) -> p z m", m=128), src),
                        WOSEM[u % 3], 16)
                    ctx.mark((5, "wopl", tc, u), WOSEM[u % 3])
                ctx.need(g, (5, "xdone", tc))
                o = ap["X"][:, :, tc * 512:(tc + 1) * 512].rearrange("c p n -> p c n")
                ctx.inc(g.dma_start(o, Xt[:].rearrange("p (c n) -> p c n", n=512)),
                        ["g7", "g8"][tc % 2], 16)
                ctx.mark((5, "store", tc), ["g7", "g8"][tc % 2])
            g.wait_ge(ctx.sem["g7"], ctx.c["g7"])
            g.wait_ge(ctx.sem["g8"], ctx.c["g8"])

        def s5_vector(v):
            ctx.need(v, (5, "aexp"))
            ctx.inc(v.tensor_scalar_mul(acol[:], aex[:], -1.0), "DVE")
            ctx.mark((5, "acol"), "DVE")
            for tc in range(ntc):
                ctx.need(v, (5, "xl", tc))
                if tc >= 1:
                    ctx.need(v, (5, "sm", tc - 1))
                    ctx.need(v, (5, "zsil", tc - 1, 15))
                    ctx.need(v, (5, "ysq", tc - 1, 15))
                for dc in range(8):
                    ctx.inc(v.tensor_copy(xbf[:, dc * 512:(dc + 1) * 512],
                                          Xt[:, dc * 512:(dc + 1) * 512]), "DVE")
                ctx.mark((5, "cp", tc), "DVE")
                ctx.need(v, (5, "sm", tc))
                ctx.inc(v.tensor_scalar_mul(r_mu, ps[0][0:1, :], 1.0 / 1024), "DVE")
                ctx.fence(v, "DVE")
                ctx.inc(v.tensor_tensor(r_m2, r_mu, r_mu, ALU.mult), "DVE")
                ctx.fence(v, "DVE")
                ctx.inc(v.scalar_tensor_tensor(r_var, ps[1][0:1, :], 1.0 / 1024,
                                               r_m2, ALU.mult, ALU.subtract), "DVE")
                ctx.mark((5, "var", tc), "DVE")
                ctx.need(v, (5, "sqrt1", tc))
                ctx.inc(v.reciprocal(r_r, r_sq), "DVE")
                ctx.mark((5, "rr", tc), "DVE")
                ctx.need(v, (5, "bc", tc))
                for dc in range(8):
                    if dc >= 1:
                        ctx.fence(v, "DVE")
                    ctx.inc(v.scalar_tensor_tensor(
                        acc0[:], Xt[:, dc * 512:(dc + 1) * 512], 1.0,
                        ps[6][:], ALU.mult, ALU.subtract), "DVE")
                    ctx.fence(v, "DVE")
                    ctx.inc(v.scalar_tensor_tensor(
                        acc1[:], acc0[:], 1.0,
                        ps[7][:], ALU.mult, ALU.mult), "DVE")
                    ctx.fence(v, "DVE")
                    ctx.inc(v.tensor_copy(gb[:, dc * 512:(dc + 1) * 512],
                                          acc1[:]), "DVE")
                    ctx.fence(v, "DVE")
                    ctx.inc(v.tensor_tensor(Xt[:, dc * 512:(dc + 1) * 512],
                                            Xt[:, dc * 512:(dc + 1) * 512],
                                            acc1[:], ALU.add), "DVE")
                ctx.mark((5, "g", tc), "DVE")
                for cc in range(18):
                    ctx.need(v, (5, "ipe", tc, 16 + cc))
                    if cc >= 1:
                        ctx.need(v, (5, "convs", tc, cc - 1))
                    base = cc * 515
                    ctx.fence(v, "DVE")
                    ctx.inc(v.tensor_scalar_mul(acc0[:], xbc[:, base:base + 512],
                                                cw[:, cc * 4:cc * 4 + 1]), "DVE")
                    accs = [acc0, acc1]
                    for k in range(1, 4):
                        ctx.fence(v, "DVE")
                        ctx.inc(v.scalar_tensor_tensor(
                            accs[k % 2][:], xbc[:, base + k:base + k + 512],
                            cw[:, cc * 4 + k:cc * 4 + k + 1],
                            accs[(k - 1) % 2][:], ALU.mult, ALU.add), "DVE")
                    ctx.mark((5, "conva", tc, cc), "DVE")
                ctx.need(v, (5, "dtevac", tc))
                ctx.fence(v, "DVE")
                ctx.inc(v.tensor_scalar_add(dw1[:], dtraw[:], dtb[:]), "DVE")
                ctx.fence(v, "DVE")
                ctx.inc(v.tensor_scalar_mul(dw2[:], dw1[:], -1.0), "DVE")
                ctx.fence(v, "DVE")
                ctx.inc(v.tensor_max(dw2[:], dw1[:], dw2[:]), "DVE")
                ctx.mark((5, "absdt", tc), "DVE")
                ctx.need(v, (5, "lnexp", tc))
                ctx.fence(v, "DVE")
                ctx.inc(v.tensor_scalar_max(dw2[:], dw1[:], 0.0), "DVE")
                ctx.fence(v, "DVE")
                ctx.inc(v.tensor_tensor(dtv[:], dw2[:], lnrow[:], ALU.add), "DVE")
                ctx.mark((5, "dt", tc), "DVE")
                ctx.need(v, (5, "acol"))
                ctx.fence(v, "DVE")
                ctx.inc(v.tensor_scalar_mul(dw1[:], dtv[:], acol[:]), "DVE")
                ctx.fence(v, "DVE")
                ctx.inc(v.tensor_tensor_scan(Wc[:], dw1[:], zzero[:], 0.0,
                                             ALU.add, ALU.add), "DVE")
                ctx.mark((5, "Wdone", tc), "DVE")
                ctx.fence(v, "DVE")
                ctx.inc(v.tensor_scalar_mul(endp[:], zzero[0:1, 0:32], 0.0), "DVE")
                ctx.mark((5, "endp0", tc), "DVE")
                for c in range(4):
                    ctx.need(v, (5, "wrow", tc, c))
                    ctx.fence(v, "DVE")
                    ctx.inc(v.tensor_copy(
                        endr[:], bass.AP(Wrow, 127, [[4096, 1], [128, 32]])), "DVE")
                    ctx.mark((5, "endr", tc, c), "DVE")
                    ctx.fence(v, "DVE")
                    ctx.inc(v.tensor_tensor(lrow[:], endr[:], endp[:],
                                            ALU.subtract), "DVE")
                    ctx.mark((5, "lsub", tc, c), "DVE")
                    for qh in range(8):
                        ctx.need(v, (5, "wtb", tc, c, qh))
                        ctx.need(v, (5, "wtc", tc, c))
                        wtv = bass.AP(WTc, qh * 4, [[32, 128], [1, 4], [0, 128]])
                        if qh >= 2:
                            ctx.need(v, (5, "eq", tc, c, qh - 2))
                        ctx.inc(v.tensor_tensor(Dsb[qh % 2][:],
                                                ps[5 + qh % 2][:], wtv,
                                                ALU.subtract), "DVE")
                        ctx.fence(v, "DVE")
                        ctx.inc(v.tensor_scalar_min(Dsb[qh % 2][:],
                                                    Dsb[qh % 2][:], 0.0), "DVE")
                        ctx.mark((5, "dq", tc, c, qh), "DVE")
                        ctx.need(v, (5, "endpe", tc, c))
                        epv = bass.AP(endc, qh * 4, [[32, 128], [1, 4], [0, 128]])
                        if qh >= 2:
                            ctx.need(v, (5, "xwq", tc, c, qh - 2))
                        ctx.inc(v.tensor_tensor(Dsb2[qh % 2][:],
                                                ps[5 + qh % 2][:], epv,
                                                ALU.subtract), "DVE")
                        ctx.mark((5, "dq2", tc, c, qh), "DVE")
                        ctx.need(v, (5, "eq", tc, c, qh))
                        ctx.need(v, (5, "gevac", tc, c))
                        triv = bass.AP(trib, 0, [[128, 128], [0, 4], [1, 128]])
                        ctx.inc(v.tensor_tensor(Ptl[:, qh * 512:(qh + 1) * 512],
                                                Ealy[qh % 2][:],
                                                triv, ALU.mult), "DVE")
                        ctx.fence(v, "DVE")
                        gv = bass.AP(Gsb, 0, [[128, 128], [0, 4], [1, 128]])
                        ctx.inc(v.tensor_tensor(Ptl[:, qh * 512:(qh + 1) * 512],
                                                Ptl[:, qh * 512:(qh + 1) * 512],
                                                gv, ALU.mult), "DVE")
                        ctx.mark((5, "ptl", tc, c, qh), "DVE")
                        ctx.need(v, (5, "xwq", tc, c, qh))
                        ctx.need(v, (5, "convs", tc, 17))
                        cs = xs[:, 17 * 512 + c * 128:17 * 512 + (c + 1) * 128]
                        cv = bass.AP(cs.tensor, cs.offset, [[18 * 512, 128],
                                                            [0, 4], [1, 128]])
                        ctx.fence(v, "DVE")
                        ctx.inc(v.tensor_tensor(Chat[:, qh * 512:(qh + 1) * 512],
                                                cv, expw[qh % 2][:],
                                                ALU.mult), "DVE")
                        ctx.mark((5, "chat", tc, c, qh), "DVE")
                    ctx.need(v, (5, "endb", tc, c))
                    ctx.fence(v, "DVE")
                    ctx.inc(v.tensor_tensor(wtl[:], ps[2][:, 0:32], WTc[:],
                                            ALU.subtract), "DVE")
                    ctx.mark((5, "wtlsub", tc, c), "DVE")
                    ctx.need(v, (5, "wtlexp", tc, c))
                    ctx.need(v, (5, "btm", tc, c))
                    for hh in range(2):
                        bv = bass.AP(Btm, 0, [[128, 128], [0, 16], [1, 128]])
                        wv2 = bass.AP(wtl, hh * 16, [[32, 128], [1, 16], [0, 128]])
                        ctx.fence(v, "DVE")
                        ctx.inc(v.tensor_tensor(
                            Bhat[:, hh * 2048:(hh + 1) * 2048], bv, wv2,
                            ALU.mult), "DVE")
                    ctx.mark((5, "bhat", tc, c), "DVE")
                    ctx.need(v, (5, "xst", tc, c))
                    ctx.need(v, (5, "dtt", tc, c))
                    dv2 = bass.AP(dtT, 0, [[32, 128], [1, 32], [0, 64]])
                    ctx.fence(v, "DVE")
                    ctx.inc(v.tensor_tensor(xdtT[:], xsT[:], dv2, ALU.mult), "DVE")
                    ctx.mark((5, "xdt", tc, c), "DVE")
                    if tc == 0 and c == 0:
                        ctx.need(v, (5, "dcole", ))
                    for i in range(16):
                        ctx.need(v, (5, "ymm", tc, c, i))
                        ctx.need(v, (5, "convs", tc, i))
                        ctx.inc(v.scalar_tensor_tensor(
                            ytc[:, i * 512 + c * 128:i * 512 + (c + 1) * 128],
                            xs[:, i * 512 + c * 128:i * 512 + (c + 1) * 128],
                            dcol[:, i:i + 1], ps[5 + i % 2][:, 0:128],
                            ALU.mult, ALU.add), "DVE")
                        ctx.mark((5, "yev", tc, c, i), "DVE")
                    ctx.need(v, (5, "lambe", tc, c))
                    ctx.fence(v, "DVE")
                    lv = bass.AP(lamb_sb, 0, [[32, 128], [1, 32], [0, 64]])
                    ctx.inc(v.tensor_tensor(ST_[:], ST_[:], lv, ALU.mult), "DVE")
                    ctx.mark((5, "stt", tc, c), "DVE")
                    for grp in range(4):
                        ctx.need(v, (5, "dmm", tc, c, grp))
                        ctx.fence(v, "DVE")
                        ctx.inc(v.tensor_tensor(ST_[:, grp * 512:(grp + 1) * 512],
                                                ST_[:, grp * 512:(grp + 1) * 512],
                                                [ps[3], ps[4]][grp % 2][:],
                                                ALU.add), "DVE")
                        ctx.mark((5, "stu", tc, c, grp), "DVE")
                        ctx.fence(v, "DVE")
                        ctx.inc(v.tensor_copy(STb[:, grp * 512:(grp + 1) * 512],
                                              ST_[:, grp * 512:(grp + 1) * 512]),
                                "DVE")
                        if grp == 3:
                            ctx.mark((5, "stb", tc, c), "DVE")
                    ctx.fence(v, "DVE")
                    ctx.inc(v.tensor_copy(endp[:], endr[:]), "DVE")
                    ctx.mark((5, "wrowfree", tc, c), "DVE")
                for cc in range(16):
                    ctx.need(v, (5, "zsil", tc, cc))
                    ctx.inc(v.tensor_tensor(ytc[:, cc * 512:(cc + 1) * 512],
                                            ytc[:, cc * 512:(cc + 1) * 512],
                                            [zsil, zsil2][cc % 2][:], ALU.mult),
                            "DVE")
                    ctx.mark((5, "ygate", tc, cc), "DVE")
                ctx.need(v, (5, "sm2", tc))
                ctx.fence(v, "DVE")
                ctx.inc(v.tensor_scalar_mul(r_s2, ps[0][0:1, :], 1.0 / 2048), "DVE")
                ctx.mark((5, "s2m", tc), "DVE")
                ctx.need(v, (5, "sqrt2", tc))
                ctx.inc(v.reciprocal(r_s2, r_sq), "DVE")
                ctx.mark((5, "r2", tc), "DVE")
                ctx.need(v, (5, "r2bc", tc))
                for cc in range(16):
                    ctx.fence(v, "DVE")
                    ctx.inc(v.scalar_tensor_tensor(
                        ytc[:, cc * 512:(cc + 1) * 512],
                        ytc[:, cc * 512:(cc + 1) * 512],
                        nw[:, cc:cc + 1], ps[6][:], ALU.mult, ALU.mult), "DVE")
                ctx.mark((5, "yn", tc), "DVE")
                for oc in range(8):
                    ctx.need(v, (5, "opmm", tc, oc))
                    ctx.inc(v.tensor_tensor(Xt[:, oc * 512:(oc + 1) * 512],
                                            Xt[:, oc * 512:(oc + 1) * 512],
                                            ps[oc % 2][:], ALU.add), "DVE")
                    ctx.mark((5, "xev", tc, oc), "DVE")
                ctx.mark((5, "xdone", tc), "DVE")

        def s5_scalar(sc):
            ctx.need(sc, (5, "const"))
            ctx.inc(sc.activation(aex[:], alog[:], AF.Exp), "ACT")
            ctx.mark((5, "aexp"), "ACT")
            ctx.need(sc, (5, "drowt"))
            ctx.inc(sc.copy(drow[:], ps[0][0:1, 0:32]), "ACT")
            ctx.mark((5, "drowe"), "ACT")
            ctx.need(sc, (5, "dcolm"))
            ctx.inc(sc.copy(dcol[:], ps[1][:, 0:16]), "ACT")
            ctx.mark((5, "dcole"), "ACT")
            for tc in range(ntc):
                ctx.need(sc, (5, "xl", tc))
                if tc >= 1:
                    ctx.need(sc, (5, "sm", tc - 1))
                    ctx.need(sc, (5, "sm2", tc - 1))
                for dc in range(8):
                    ctx.inc(sc.activation(xsq[:, dc * 512:(dc + 1) * 512],
                                          Xt[:, dc * 512:(dc + 1) * 512],
                                          AF.Square), "ACT")
                ctx.mark((5, "sq", tc), "ACT")
                ctx.need(sc, (5, "var", tc))
                ctx.inc(sc.activation(r_sq, r_var, AF.Sqrt,
                                      bias=ctx.constf(1e-5)[0:1, :]), "ACT")
                ctx.mark((5, "sqrt1", tc), "ACT")
                if tc >= 1:
                    ctx.need(sc, (5, "convd", tc - 1))
                    for cc in range(18):
                        ctx.inc(sc.copy(xbc[:, cc * 515:cc * 515 + 3],
                                        xbc[:, cc * 515 + 512:cc * 515 + 515]),
                                "ACT")
                ctx.mark((5, "halo", tc), "ACT")
                for oc in range(NOC):
                    ctx.need(sc, (5, "ipm", tc, oc))
                    if oc < 16:
                        ctx.need(sc, (5, "sm", tc))
                        ctx.inc(sc.copy(ztc[:, oc * 512:(oc + 1) * 512],
                                        ps[oc % 2][:]), "ACT")
                    elif oc < 34:
                        cc = oc - 16
                        ctx.inc(sc.copy(xbc[:, cc * 515 + 3:cc * 515 + 515],
                                        ps[oc % 2][:]), "ACT")
                    else:
                        ctx.inc(sc.copy(dtraw[:], ps[oc % 2][0:32, :]), "ACT")
                        ctx.mark((5, "dtevac", tc), "ACT")
                    ctx.mark((5, "ipe", tc, oc), "ACT")
                for cc in range(18):
                    ctx.need(sc, (5, "conva", tc, cc))
                    ctx.inc(sc.activation(xs[:, cc * 512:(cc + 1) * 512],
                                          acc1[:], AF.Silu,
                                          bias=cb[:, cc:cc + 1]), "ACT")
                    ctx.mark((5, "convs", tc, cc), "ACT")
                ctx.mark((5, "convd", tc), "ACT")
                ctx.need(sc, (5, "absdt", tc))
                ctx.need(sc, (5, "acol"))
                ctx.inc(sc.activation(lnrow[:], dw2[:], AF.Exp, scale=-1.0), "ACT")
                ctx.fence(sc, "ACT")
                ctx.inc(sc.activation(lnrow[:], lnrow[:], AF.Ln,
                                      bias=ctx.constf(1.0)[0:32, :]), "ACT")
                ctx.mark((5, "lnexp", tc), "ACT")
                for c in range(4):
                    for i in range(16):
                        ctx.need(sc, (5, "trx", tc, c, i))
                        ctx.inc(sc.copy(xsT[:, i * 128:(i + 1) * 128],
                                        [ps[0], ps[1]][i % 2][0:128, 0:128]), "ACT")
                        ctx.mark((5, "xstp", tc, c, i), "ACT")
                        if ctx.recording and i == 15:
                            ctx.sched[(5, "xst", tc, c)] = \
                                ctx.sched[(5, "xstp", tc, c, 15)]
                    ctx.need(sc, (5, "trb", tc, c))
                    ctx.inc(sc.copy(Btm[:], ps[0][0:128, 128:256]), "ACT")
                    ctx.mark((5, "btm", tc, c), "ACT")
                    ctx.need(sc, (5, "trw", tc, c))
                    ctx.inc(sc.copy(WTc[:], ps[1][0:128, 128:160]), "ACT")
                    ctx.mark((5, "wtc", tc, c), "ACT")
                    ctx.need(sc, (5, "trd", tc, c))
                    ctx.inc(sc.copy(dtT[:], ps[0][0:128, 256:288]), "ACT")
                    ctx.mark((5, "dtt", tc, c), "ACT")
                    ctx.need(sc, (5, "gmm", tc, c))
                    ctx.inc(sc.copy(Gsb[:], ps[2][:, 0:128]), "ACT")
                    ctx.mark((5, "gevac", tc, c), "ACT")
                    ctx.need(sc, (5, "endpb", tc, c))
                    ctx.inc(sc.copy(endc[:], ps[2][:, 32:64]), "ACT")
                    ctx.mark((5, "endpe", tc, c), "ACT")
                    for qh in range(8):
                        ctx.need(sc, (5, "dq", tc, c, qh))
                        if qh >= 2:
                            ctx.need(sc, (5, "ptl", tc, c, qh - 2))
                        ctx.inc(sc.activation(Ealy[qh % 2][:], Dsb[qh % 2][:],
                                              AF.Exp), "ACT")
                        ctx.mark((5, "eq", tc, c, qh), "ACT")
                        ctx.need(sc, (5, "dq2", tc, c, qh))
                        if qh >= 2:
                            ctx.need(sc, (5, "chat", tc, c, qh - 2))
                        ctx.inc(sc.activation(expw[qh % 2][:], Dsb2[qh % 2][:],
                                              AF.Exp), "ACT")
                        ctx.mark((5, "xwq", tc, c, qh), "ACT")
                    ctx.need(sc, (5, "wtlsub", tc, c))
                    ctx.inc(sc.activation(wtl[:], wtl[:], AF.Exp), "ACT")
                    ctx.mark((5, "wtlexp", tc, c), "ACT")
                    ctx.need(sc, (5, "lsub", tc, c))
                    ctx.inc(sc.activation(lrow[:], lrow[:], AF.Exp), "ACT")
                    ctx.mark((5, "lexp", tc, c), "ACT")
                    ctx.need(sc, (5, "lambm", tc, c))
                    ctx.inc(sc.copy(lamb_sb[:], ps[2][:, 0:32]), "ACT")
                    ctx.mark((5, "lambe", tc, c), "ACT")
                for cc in range(16):
                    if cc >= 2:
                        ctx.need(sc, (5, "ygate", tc, cc - 2))
                    ctx.inc(sc.activation([zsil, zsil2][cc % 2][:],
                                          ztc[:, cc * 512:(cc + 1) * 512],
                                          AF.Silu), "ACT")
                    ctx.mark((5, "zsil", tc, cc), "ACT")
                ctx.fence(sc, "ACT")
                for cc in range(16):
                    ctx.need(sc, (5, "ygate", tc, cc))
                    if cc >= 8:
                        ctx.need(sc, (5, "sm2p", tc, cc - 8))
                    ctx.inc(sc.activation(xsq[:, (cc % 8) * 512:(cc % 8 + 1) * 512],
                                          ytc[:, cc * 512:(cc + 1) * 512],
                                          AF.Square), "ACT")
                    ctx.mark((5, "ysq", tc, cc), "ACT")
                ctx.need(sc, (5, "s2m", tc))
                ctx.inc(sc.activation(r_sq, r_s2, AF.Sqrt,
                                      bias=ctx.constf(1e-5)[0:1, :]), "ACT")
                ctx.mark((5, "sqrt2", tc), "ACT")

        def s5_tensor(t):
            ctx.need(t, (5, "const"))
            ctx.inc(t.transpose(ps[0][0:1, 0:32], d32[:], ctx.cident_f[0:32, 0:32]), "PE")
            ctx.mark((5, "drowt"), "PE")
            ctx.need(t, (5, "drowe"))
            ctx.inc(t.matmul(ps[1][:, 0:16], ctx.ch0,
                             bass.AP(drow, 0, [[32, 1], [2, 16]]),
                             start=True, stop=False), "PE")
            ctx.inc(t.matmul(ps[1][:, 0:16], ctx.ch1,
                             bass.AP(drow, 1, [[32, 1], [2, 16]]),
                             start=False, stop=True), "PE")
            ctx.mark((5, "dcolm"), "PE")
            for tc in range(ntc):
                ctx.need(t, (5, "cp", tc))
                ctx.need(t, (5, "sq", tc))
                if tc == 0:
                    ctx.need(t, (5, "drowe"))
                    ctx.need(t, (5, "dcole"))
                else:
                    ctx.need(t, (5, "var", tc - 1))
                    ctx.need(t, (5, "s2m", tc - 1))
                    ctx.need(t, (5, "xstp", tc - 1, 3, 15))
                    ctx.need(t, (5, "btm", tc - 1, 3))
                    ctx.need(t, (5, "dtt", tc - 1, 3))
                    ctx.need(t, (5, "xev", tc - 1, 7))
                for dc in range(8):
                    ctx.inc(t.matmul(ps[0][0:1, :], ctx.ones_b,
                                     xbf[:, dc * 512:(dc + 1) * 512],
                                     start=(dc == 0), stop=(dc == 7)), "PE")
                    ctx.inc(t.matmul(ps[1][0:1, :], ctx.ones_b,
                                     xsq[:, dc * 512:(dc + 1) * 512],
                                     start=(dc == 0), stop=(dc == 7)), "PE")
                ctx.mark((5, "sm", tc), "PE")
                ctx.need(t, (5, "var", tc))
                if tc >= 1:
                    ctx.need(t, (5, "g", tc - 1))
                    ctx.need(t, (5, "yev", tc - 1, 3, 15))
                    ctx.need(t, (5, "yn", tc - 1))
                ctx.inc(t.matmul(ps[6][:], ctx.ones_f, r_mu,
                                 start=True, stop=True), "PE")
                ctx.need(t, (5, "rr", tc))
                ctx.inc(t.matmul(ps[7][:], ctx.ones_f, r_r,
                                 start=True, stop=True), "PE")
                ctx.mark((5, "bc", tc), "PE")
                ctx.need(t, (5, "g", tc))
                for oc in range(NOC):
                    ctx.need(t, (5, "wipl", tc, oc))
                    if tc > 0 or oc >= 2:
                        pm = (5, "ipe", tc, oc - 2) if oc >= 2 \
                            else (5, "ipe", tc - 1, oc + NOC - 2)
                        ctx.need(t, pm)
                    elif tc == 0 and oc < 2:
                        pass
                    mw = 128 if oc < 34 else 32
                    out = ps[oc % 2][:] if oc < 34 else ps[oc % 2][0:32, :]
                    for dc in range(8):
                        ctx.inc(t.matmul(out,
                                         wip[oc % 2][:, dc * 128:dc * 128 + mw],
                                         gb[:, dc * 512:(dc + 1) * 512],
                                         start=(dc == 0), stop=(dc == 7)), "PE")
                    ctx.mark((5, "ipm", tc, oc), "PE")
                for c in range(4):
                    for i in range(16):
                        ctx.need(t, (5, "convs", tc, i))
                        if c > 0 or i >= 2 or tc > 0:
                            pm = (5, "xstp", tc, c, i - 2) if i >= 2 else \
                                ((5, "xstp", tc, c - 1, i + 14) if c > 0 else
                                 (5, "xstp", tc - 1, 3, i + 14))
                            ctx.need(t, pm)
                        if i == 0:
                            if c == 0:
                                ctx.need(t, (5, "ipe", tc, NOC - 1))
                            pb = (tc, c - 1) if c > 0 else (tc - 1, 3)
                            if c > 0 or tc > 0:
                                ctx.need(t, (5, "btm") + pb)
                                ctx.need(t, (5, "dtt") + pb)
                        if i == 1 and (c > 0 or tc > 0):
                            pb = (tc, c - 1) if c > 0 else (tc - 1, 3)
                            ctx.need(t, (5, "wtc") + pb)
                        ctx.inc(t.matmul(
                            [ps[0], ps[1]][i % 2][0:128, 0:128],
                            xs[:, i * 512 + c * 128:i * 512 + (c + 1) * 128],
                            ctx.cident_b, start=True, stop=True), "PE")
                        ctx.mark((5, "trx", tc, c, i), "PE")
                    ctx.need(t, (5, "convs", tc, 16))
                    ctx.need(t, (5, "xstp", tc, c, 14))
                    if c > 0 or tc > 0:
                        ctx.need(t, (5, "btm", tc, c - 1) if c > 0
                                 else (5, "btm", tc - 1, 3))
                    ctx.inc(t.matmul(
                        ps[0][0:128, 128:256],
                        xs[:, 16 * 512 + c * 128:16 * 512 + (c + 1) * 128],
                        ctx.cident_b, start=True, stop=True), "PE")
                    ctx.mark((5, "trb", tc, c), "PE")
                    ctx.need(t, (5, "Wdone", tc))
                    ctx.need(t, (5, "xstp", tc, c, 15))
                    if c > 0 or tc > 0:
                        ctx.need(t, (5, "wtc", tc, c - 1) if c > 0
                                 else (5, "wtc", tc - 1, 3))
                    ctx.inc(t.transpose(ps[1][0:128, 128:160],
                                        Wc[0:32, c * 128:(c + 1) * 128],
                                        ctx.cident_f[0:32, 0:32]), "PE")
                    ctx.mark((5, "trw", tc, c), "PE")
                    ctx.need(t, (5, "dt", tc))
                    ctx.need(t, (5, "btm", tc, c))
                    if c > 0 or tc > 0:
                        ctx.need(t, (5, "dtt", tc, c - 1) if c > 0
                                 else (5, "dtt", tc - 1, 3))
                    ctx.inc(t.transpose(ps[0][0:128, 256:288],
                                        dtv[0:32, c * 128:(c + 1) * 128],
                                        ctx.cident_f[0:32, 0:32]), "PE")
                    ctx.mark((5, "trd", tc, c), "PE")
                    ctx.need(t, (5, "convs", tc, 17))
                    if c > 0 or tc > 0:
                        pb = (tc, c - 1) if c > 0 else (tc - 1, 3)
                        ctx.need(t, (5, "lambe") + pb)
                    ctx.inc(t.matmul(ps[2][:, 0:128],
                                     xs[:, 16 * 512 + c * 128:16 * 512 + (c + 1) * 128],
                                     xs[:, 17 * 512 + c * 128:17 * 512 + (c + 1) * 128],
                                     start=True, stop=True), "PE")
                    ctx.mark((5, "gmm", tc, c), "PE")
                    ctx.need(t, (5, "endr", tc, c))
                    ctx.need(t, (5, "gevac", tc, c))
                    ctx.inc(t.matmul(ps[2][:, 0:32], ctx.ones_f, endr[:],
                                     start=True, stop=True), "PE")
                    ctx.mark((5, "endb", tc, c), "PE")
                    if c == 0:
                        ctx.need(t, (5, "endp0", tc))
                    else:
                        ctx.need(t, (5, "wrowfree", tc, c - 1))
                    ctx.inc(t.matmul(ps[2][:, 32:64], ctx.ones_f, endp[:],
                                     start=True, stop=True), "PE")
                    ctx.mark((5, "endpb", tc, c), "PE")
                    for qh in range(8):
                        ctx.need(t, (5, "wrow", tc, c))
                        if qh >= 2:
                            ctx.need(t, (5, "dq", tc, c, qh - 2))
                            ctx.need(t, (5, "dq2", tc, c, qh - 2))
                        elif c > 0:
                            ctx.need(t, (5, "yev", tc, c - 1, 14 + qh))
                        elif tc > 0:
                            ctx.need(t, (5, "yev", tc - 1, 3, 14 + qh))
                        ctx.inc(t.matmul(ps[5 + qh % 2][:], ctx.ones_f,
                                         Wrow[:, qh * 512:(qh + 1) * 512],
                                         start=True, stop=True), "PE")
                        ctx.mark((5, "wtb", tc, c, qh), "PE")
                    if tc == 0 and c == 0:
                        ctx.need(t, (5, "init"))
                    else:
                        ctx.need(t, (5, "stb", tc, c - 1) if c > 0
                                 else (5, "stb", tc - 1, 3))
                    for i in range(16):
                        h0, h1 = 2 * i, 2 * i + 1
                        ctx.need(t, (5, "xdt", tc, c))
                        ctx.need(t, (5, "ptl", tc, c, (h1 * 128) // 512))
                        ctx.need(t, (5, "chat", tc, c, (h1 * 128) // 512))
                        if i < 2:
                            ctx.need(t, (5, "dq", tc, c, 7))
                            ctx.need(t, (5, "dq2", tc, c, 7))
                        if c > 0 or i >= 2 or tc > 0:
                            pm = (5, "yev", tc, c, i - 2) if i >= 2 else \
                                ((5, "yev", tc, c - 1, i + 14) if c > 0 else
                                 (5, "yev", tc - 1, 3, i + 14))
                            ctx.need(t, pm)
                        for hh, h in ((0, h0), (1, h1)):
                            tp = (0, 0) if hh == 0 else (0, 64)
                            ctx.inc(t.matmul(
                                ps[5 + i % 2][hh * 64:(hh + 1) * 64, 0:128],
                                xdtT[:, h * 64:(h + 1) * 64],
                                Ptl[:, h * 128:(h + 1) * 128],
                                start=True, stop=False,
                                tile_position=tp), "PE")
                            ctx.inc(t.matmul(
                                ps[5 + i % 2][hh * 64:(hh + 1) * 64, 0:128],
                                STb[:, h * 64:(h + 1) * 64],
                                Chat[:, h * 128:(h + 1) * 128],
                                start=False, stop=True,
                                tile_position=tp), "PE")
                        ctx.mark((5, "ymm", tc, c, i), "PE")
                    ctx.need(t, (5, "lexp", tc, c))
                    ctx.need(t, (5, "wtlsub", tc, c))
                    ctx.need(t, (5, "endpe", tc, c))
                    ctx.inc(t.matmul(ps[2][:, 0:32], ctx.ones_f, lrow[:],
                                     start=True, stop=True), "PE")
                    ctx.mark((5, "lambm", tc, c), "PE")
                    for grp in range(4):
                        bank = [ps[3], ps[4]][grp % 2]
                        ctx.need(t, (5, "bhat", tc, c))
                        ctx.need(t, (5, "xdt", tc, c))
                        if grp >= 2:
                            ctx.need(t, (5, "stu", tc, c, grp - 2))
                        elif c > 0:
                            ctx.need(t, (5, "stu", tc, c - 1, grp + 2))
                        elif tc > 0:
                            ctx.need(t, (5, "stu", tc - 1, 3, grp + 2))
                        for j in range(8):
                            h = grp * 8 + j
                            ctx.inc(t.matmul(
                                bank[:, j * 64:(j + 1) * 64],
                                Bhat[:, h * 128:(h + 1) * 128],
                                xdtT[:, h * 64:(h + 1) * 64],
                                start=True, stop=True), "PE")
                        ctx.mark((5, "dmm", tc, c, grp), "PE")
                for cc in range(16):
                    ctx.need(t, (5, "ysq", tc, cc))
                    ctx.inc(t.matmul(ps[0][0:1, :], ctx.ones_b,
                                     xsq[:, (cc % 8) * 512:(cc % 8 + 1) * 512],
                                     start=(cc == 0), stop=(cc == 15)), "PE")
                    ctx.mark((5, "sm2p", tc, cc), "PE")
                ctx.mark((5, "sm2", tc), "PE")
                ctx.need(t, (5, "r2", tc))
                ctx.inc(t.matmul(ps[6][:], ctx.ones_f, r_s2,
                                 start=True, stop=True), "PE")
                ctx.mark((5, "r2bc", tc), "PE")
                ctx.need(t, (5, "yn", tc))
                for oc in range(8):
                    if tc > 0 or oc >= 2:
                        pm = (5, "xev", tc, oc - 2) if oc >= 2 \
                            else (5, "xev", tc - 1, oc + 6)
                        ctx.need(t, pm)
                    for half in range(2):
                        u = oc * 2 + half
                        ctx.need(t, (5, "wopl", tc, u))
                        for j in range(8):
                            yc = half * 8 + j
                            ctx.inc(t.matmul(ps[oc % 2][:],
                                             woph[u % 3][:, j * 128:(j + 1) * 128],
                                             ytc[:, yc * 512:(yc + 1) * 512],
                                             start=(yc == 0), stop=(yc == 15)), "PE")
                        ctx.mark((5, "opu", tc, u), "PE")
                    ctx.mark((5, "opmm", tc, oc), "PE")

        run_two_pass(ctx, [("sync", s5_sync), ("gpsimd", s5_gpsimd),
                           ("vector", s5_vector), ("scalar", s5_scalar),
                           ("tensor", s5_tensor)])
    barrier(ctx, "p5")



def build_program(T=4096, phases=("p0", "p7"), prep_moe=True, prep_l1=True):
    nc = bass.Bass("TRN2", target_bir_lowering=False)
    ctx = Ctx(nc, T)
    ap = ctx.ap

    def din(name, shape, dt):
        ap[name] = nc.dram_tensor(name, list(shape), dt, kind="ExternalInput").ap()

    din("x_shard", (T, 1024), F32)
    din("L0_attn_W_qkv", (1024, 3072), F32)
    din("L0_attn_b_qkv", (3072,), F32)
    din("L0_attn_ln_w", (1024,), F32)
    din("L0_attn_ln_b", (1024,), F32)
    din("L0_ffn_ln1_w", (1024,), F32)
    din("L0_ffn_ln2_w", (1024,), F32)
    din("L0_ffn_wq", (1024, 1024), F32)
    din("L0_ffn_wk", (1024, 512), F32)
    din("L0_ffn_wv", (1024, 512), F32)
    din("L0_ffn_wo", (1024, 1024), F32)
    din("L0_ffn_router_w", (1024, 8), F32)
    din("L0_ffn_w_gate", (8, 1024, 3584), F32)
    din("L0_ffn_w_up", (8, 1024, 3584), F32)
    din("L0_ffn_w_down", (8, 3584, 1024), F32)
    din("L1_attn_in_proj_w", (4384, 1024), F32)
    din("L1_attn_conv_w", (2304, 4), F32)
    din("L1_attn_conv_b", (2304,), F32)
    din("L1_attn_dt_bias", (32,), F32)
    din("L1_attn_A_log", (32,), F32)
    din("L1_attn_D", (32,), F32)
    din("L1_attn_norm_w", (2048,), F32)
    din("L1_attn_out_proj_w", (1024, 2048), F32)
    din("L1_ffn_w_gate", (3072, 1024), I32)
    din("L1_ffn_w_up", (3072, 1024), I32)
    din("L1_ffn_w_down", (1024, 3072), I32)
    for cn, (shp, cdt) in _const_shapes(T).items():
        din(cn, shp, cdt)

    ap["out_shard"] = nc.dram_tensor("out_shard", [T, 1024], F32,
                                     kind="ExternalOutput").ap()

    def scr(name, shape, dt):
        ap[name] = nc.dram_tensor(name, list(shape), dt, kind="Internal").ap()

    scr("X", (8, 128, T), F32)
    scr("X1", (8, 128, T), F32)
    if prep_moe:
        scr("s_g", (8, 1024, 3584), BF16)
        scr("s_u", (8, 1024, 3584), BF16)
        scr("s_d", (8, 3584, 1024), BF16)
        scr("s_wqc", (1024, 1024), BF16)
        scr("s_woc", (1024, 1024), BF16)
        scr("s_wkc", (1024, 512), BF16)
        scr("s_wvc", (1024, 512), BF16)
    if prep_l1:
        scr("s_wg", (3072, 1024), BF16)
        scr("s_wu", (3072, 1024), BF16)
        scr("s_wd", (1024, 3072), BF16)
        scr("s_wgT", (1024, 3072), BF16)
        scr("s_wuT", (1024, 3072), BF16)
        scr("s_wdT", (3072, 1024), BF16)
        scr("s_ip", (4384, 1024), BF16)
        scr("s_ipT", (1024, 4384), BF16)
        scr("s_op", (1024, 2048), BF16)
        scr("s_opT", (2048, 1024), BF16)
    scr("s_small", (64, 4096), F32)

    with contextlib.ExitStack() as stack:
        ctx.stack = stack
        for s in SEMS:
            ctx.mksem(s)
        ctx.cident_f = stack.enter_context(
            nc.sbuf_tensor("cidentf", [128, 128], F32))[:]
        ctx.cident_b = stack.enter_context(
            nc.sbuf_tensor("cidentb", [128, 128], BF16))[:]
        ctx.ones_f = stack.enter_context(
            nc.sbuf_tensor("onesf", [1, 128], F32))[:]
        ctx.ones_b = stack.enter_context(
            nc.sbuf_tensor("onesb", [128, 1], BF16))[:]
        ctx.cscal = stack.enter_context(
            nc.sbuf_tensor("cscal", [128, len(C_VALS)], F32))[:]
        ctx.ch0 = stack.enter_context(
            nc.sbuf_tensor("ch0", [1, 128], F32))[:]
        ctx.ch1 = stack.enter_context(
            nc.sbuf_tensor("ch1", [1, 128], F32))[:]
        with nc.Block() as block:
            @block.gpsimd
            def _(g):
                ctx.inc(g.dma_start(ctx.cident_f, ap["c_identf"]), "g0", 16)
                ctx.inc(g.dma_start(ctx.cident_b, ap["c_identb"]), "g1", 16)
                ctx.inc(g.dma_start(ctx.cscal, ap["c_scal"]), "g2", 16)
                g.memset(ctx.ones_f, 1.0)
                g.memset(ctx.ones_b, 1.0)
                g.wait_ge(ctx.sem["g0"], ctx.c["g0"])
                g.wait_ge(ctx.sem["g1"], ctx.c["g1"])
                g.wait_ge(ctx.sem["g2"], ctx.c["g2"])
        barrier(ctx, "init")

        ctx.cast_moe_at = "mA" if (prep_moe and "l0ffnA" in phases) else None
        if "p0" in phases:
            p0_prep(ctx, prep_moe=prep_moe, prep_l1=prep_l1)
        if "l0ffnA" in phases:
            p_l0ffn(ctx, "mA", parts=globals().get("_L0FFN_PARTS", "ABC"))
        if "l0attn" in phases:
            p_l0attn(ctx)
            barrier(ctx, "l0a")
        if "l0ffnB" in phases:
            p_l0ffn(ctx, "mB")
        if "l1ffnA" in phases:
            p_l1ffn(ctx, "fA")
            barrier(ctx, "fA")
        if "mamba" in phases:
            p_mamba(ctx)
        if "l1ffnB" in phases:
            p_l1ffn(ctx, "fB")
            barrier(ctx, "fB")
        if "p7" in phases:
            _stream_transpose(ctx, src_is_tokens=False, tag="o")
    nc._ctx_sched = dict(ctx.sched)
    return nc



def kernel(**inputs):
    T = 4096
    x = np.ascontiguousarray(np.asarray(inputs["L0_attn_x"], dtype=np.float32))
    consts = _host_consts(T)
    nc = build_program(T=T, phases=("p0", "l0ffnA", "l0attn", "l0ffnB",
                                    "l1ffnA", "mamba", "l1ffnB", "p7"),
                       prep_moe=True, prep_l1=True)
    base = {}
    for k, v in inputs.items():
        if k == "L0_attn_x":
            continue
        base[k] = np.ascontiguousarray(np.asarray(v))
    for k, v in consts.items():
        base[k] = np.ascontiguousarray(v)
    in_maps = []
    for c in range(8):
        m = dict(base)
        m["x_shard"] = np.ascontiguousarray(x[c])
        in_maps.append(m)
    res = run_bass_kernel_spmd(nc, in_maps, core_ids=list(range(8)))
    out = np.stack([np.asarray(r["out_shard"]) for r in res.results], axis=0)
    return out.astype(np.float32)
